# version 05
# baseline (speedup 1.0000x reference)
import contextlib
from collections import defaultdict

import numpy as np

import concourse.bass as bass
import concourse.mybir as mybir
from concourse.bass_utils import run_bass_kernel_spmd

F32 = mybir.dt.float32
BF16 = mybir.dt.bfloat16
I32 = mybir.dt.int32
AF = mybir.ActivationFunctionType
AX = mybir.AxisListType
ALU = mybir.AluOpType

NCORE = 8
T = 8192
D = 4096
TPC = 1024
NT = TPC // 128
NH = 32
HD = 128
NKC = D // 128
QT = 512
LA, LB = 32, 64
NSLOT = LA + LB + 8
SCALE_ATT = 1.0 / float(np.sqrt(np.float32(128.0)))
EPS_LN = 1e-5
EPS_RMS = 1.1920929e-07
SINK_IT = 50
EXPC0 = 15.0
EXPC1 = 5.0

ENGS = ("tensor", "vector", "scalar", "gpsimd", "sync")

PLAN = [["p1", "p2", "p3a", "p3b", "p4", "p5"]]
RACE_DETECT = True
DEBUG_P3B = False
DEBUG_P5 = False
P5_PARTS = "pfa"
P5F_SERIAL = False


class _RInstr:
    __slots__ = ("meth", "args", "kwargs", "incs")

    def __init__(self, meth, args, kwargs):
        self.meth = meth
        self.args = args
        self.kwargs = kwargs
        self.incs = []


class _Handle:
    __slots__ = ("p", "ri")

    def __init__(self, p, ri):
        self.p = p
        self.ri = ri

    def inc(self, sem, n=1):
        if not sem.startswith("D"):
            return self
        self.p.count[sem] += n
        self.ri.incs.append((sem, n, self.p.count[sem]))
        return self


_TRACKED = {
    "tensor_copy", "tensor_tensor", "tensor_add", "tensor_sub", "tensor_mul",
    "tensor_max", "tensor_scalar", "tensor_single_scalar", "tensor_scalar_add",
    "tensor_scalar_sub", "tensor_scalar_mul", "tensor_scalar_max",
    "tensor_scalar_min", "tensor_relu", "scalar_tensor_tensor", "tensor_reduce",
    "reduce_sum", "reduce_max", "reciprocal", "select", "transpose", "memset",
    "memzero", "activation", "iota", "copy", "add", "mul", "sqrt", "square",
    "matmul", "ldweights",
}
_ESEM = {"tensor": "Et", "vector": "Ev", "scalar": "Es", "gpsimd": "Eg",
         "sync": "Ey"}


def _ap_span(ap):
    try:
        tname = ap.tensor.name
        esz = {mybir.dt.float32: 4, mybir.dt.int32: 4,
               mybir.dt.bfloat16: 2}.get(ap.dtype, 4)
        lo = ap.offset
        hi = ap.offset
        for step, count in ap.ap:
            if count > 1 and step > 0:
                hi += step * (count - 1)
        return (tname, lo * esz, (hi + 1) * esz)
    except Exception:
        return None


class _Eng:
    def __init__(self, p, name):
        object.__setattr__(self, "_p", p)
        object.__setattr__(self, "_name", name)

    def __getattr__(self, meth):
        p = self._p
        name = self._name

        def call(*a, **k):
            if meth in _TRACKED:
                outs, ins_ = [], []
                for idx, arg in enumerate(a):
                    sp = _ap_span(arg) if hasattr(arg, "tensor") else None
                    if sp is not None:
                        if idx == 0 and meth != "ldweights":
                            outs.append(sp)
                        else:
                            ins_.append(sp)
                for kw, arg in k.items():
                    sp = _ap_span(arg) if hasattr(arg, "tensor") else None
                    if sp is not None:
                        (outs if kw == "accum_out" else ins_).append(sp)
                if meth == "matmul" and a and k.get("start") is not True:
                    sp = _ap_span(a[0])
                    if sp is not None:
                        ins_.append(sp)
                p.auto_edges(name, ins_, outs)
                ri = _RInstr(meth, a, k)
                p.q[name].append(ri)
                p.note_access(name, ins_, outs)
                return _Handle(p, ri)
            if meth == "dma_start":
                out_sp = _ap_span(k.get("out"))
                in_sp = _ap_span(k.get("in_"))
                p.auto_edges(name, [in_sp] if in_sp else [],
                             [out_sp] if out_sp else [])
                ri = _RInstr(meth, a, k)
                p.q[name].append(ri)
                p.pending_dma[name] = (ri, in_sp, out_sp)
                return _Handle(p, ri)
            if meth == "drain":
                p.pending[name].clear()
            ri = _RInstr(meth, a, k)
            p.q[name].append(ri)
            return _Handle(p, ri)

        return call


class Prog:
    def __init__(self):
        self.q = {e: [] for e in ENGS}
        self.count = defaultdict(int)
        self.hw = defaultdict(int)
        self.pending = {e: [] for e in ENGS}
        self.pending_dma = {e: None for e in ENGS}
        self.lastw = defaultdict(list)
        self.lastr = defaultdict(list)
        self.t = _Eng(self, "tensor")
        self.v = _Eng(self, "vector")
        self.s = _Eng(self, "scalar")
        self.g = _Eng(self, "gpsimd")
        self.y = _Eng(self, "sync")

    def _hits(self, store, sp):
        tn, lo, hi = sp
        return [e for e in store[tn] if e[0] < hi and lo < e[1]]

    def auto_edges(self, eng, ins_, outs):
        need_drain = False
        for sp in ins_:
            for (lo, hi, sem, cnt, e2) in self._hits(self.lastw, sp):
                if e2 == eng and sem in _ESEM.values():
                    if eng != "tensor":
                        need_drain = True
                else:
                    self.wait(eng, sem, cnt)
        for sp in outs:
            for (lo, hi, sem, cnt, e2) in self._hits(self.lastw, sp) +                     self._hits(self.lastr, sp):
                if e2 == eng and sem in _ESEM.values():
                    pass
                else:
                    self.wait(eng, sem, cnt)
        if need_drain:
            self.q[eng].append(_RInstr("drain", (), {}))

    def note_access(self, eng, ins_, outs, sem=None, amt=1):
        if sem is None:
            sem = _ESEM[eng]
        if outs:
            self.count[sem] += amt
            ri = self.q[eng][-1]
            assert isinstance(ri, _RInstr)
            ri.incs.append((sem, amt, self.count[sem]))
        cnt = self.count[sem]
        for sp in outs:
            tn, lo, hi = sp
            keep = [e for e in self.lastw[tn] if not (e[0] < hi and lo < e[1]
                                                      )]
            keep.append((lo, hi, sem, cnt, eng))
            self.lastw[tn] = keep[-4096:]
            self.lastr[tn] = [e for e in self.lastr[tn]
                              if not (e[0] < hi and lo < e[1])]
        for sp in ins_:
            tn, lo, hi = sp
            v = cnt if outs else self.count[sem]
            lst = self.lastr[tn]
            for i in range(len(lst) - 1, -1, -1):
                e = lst[i]
                if e[0] == lo and e[1] == hi and e[2] == sem and e[4] == eng:
                    del lst[i]
            lst.append((lo, hi, sem, v, eng))
            if len(lst) > 4096:
                del lst[0:1024]

    _MANUAL_DEAD = {"Ts", "Vs", "Ss", "Gs", "Ys"}

    def wait(self, eng, sem, val=None):
        if sem in self._MANUAL_DEAD:
            return
        if val is None:
            val = self.count[sem]
        if val <= 0 or val <= self.hw[(eng, sem)]:
            return
        self.hw[(eng, sem)] = val
        self.q[eng].append(("wait", sem, val))

    def snap(self, sem):
        return self.count[sem]

    def lastinc(self, eng, sem, n=1):
        if not sem.startswith("D"):
            return
        ri = self.q[eng][-1]
        assert isinstance(ri, _RInstr)
        self.count[sem] += n
        ri.incs.append((sem, n, self.count[sem]))

    def build(self, nc, es):
        import bisect
        esems = set(_ESEM.values())
        waited = defaultdict(set)
        for e in ENGS:
            for it in self.q[e]:
                if isinstance(it, tuple) and it[1] in esems:
                    waited[it[1]].add(it[2])
        wl = {sname: sorted(v) for sname, v in waited.items()}

        def rank(sname, v):
            return bisect.bisect_right(wl.get(sname, []), v)

        newcount = defaultdict(int)
        for e in ENGS:
            nq = []
            for it in self.q[e]:
                if isinstance(it, tuple):
                    _, sname, val = it
                    if sname in esems:
                        nv = rank(sname, val)
                        if nv > 0:
                            nq.append(("wait", sname, nv))
                    else:
                        nq.append(it)
                else:
                    kept = []
                    for (sname, n, ca) in it.incs:
                        if sname in esems:
                            n2 = rank(sname, ca) - rank(sname, ca - n)
                            if n2 > 0:
                                kept.append((sname, n2))
                                newcount[sname] = max(newcount[sname],
                                                      rank(sname, ca))
                        else:
                            kept.append((sname, n))
                            newcount[sname] += n
                    it.incs = kept
                    nq.append(it)
            self.q[e] = nq
        for name, c in self.count.items():
            if name not in esems:
                assert c < 60000, f"sem {name} count {c} too high"
        for name, c in newcount.items():
            assert c < 60000, f"sem {name} count {c} too high"
        used = set()
        for e in ENGS:
            for it in self.q[e]:
                if isinstance(it, tuple):
                    used.add(it[1])
                else:
                    for t in it.incs:
                        used.add(t[0])
        sems = {name: es.enter_context(nc.semaphore(name))
                for name in self.count.keys() if name in used}
        blk = es.enter_context(nc.Block())

        def make_replay(entries):
            def replay(eng_obj):
                for it in entries:
                    if isinstance(it, tuple):
                        _, sem, val = it
                        eng_obj.wait_ge(sems[sem], val)
                    else:
                        inst = getattr(eng_obj, it.meth)(*it.args, **it.kwargs)
                        for tup in it.incs:
                            inst.then_inc(sems[tup[0]], tup[1])
            return replay

        for e in ENGS:
            if self.q[e]:
                getattr(blk, e)(make_replay(self.q[e]))


def prep_host(inputs):
    f32 = np.float32
    x = np.ascontiguousarray(np.asarray(inputs["L0_attn_x"], f32).reshape(T, D))
    pre = {}
    rows = []
    for c in range(NCORE):
        a, bq = c, 15 - c
        rows.append(np.ascontiguousarray(np.concatenate(
            [x[a * 512:(a + 1) * 512], x[bq * 512:(bq + 1) * 512]], axis=0)))
    pre["x_own"] = rows

    pre["wk"] = np.ascontiguousarray(np.asarray(inputs["L0_attn_wk"], f32))
    pre["wv"] = np.ascontiguousarray(np.asarray(inputs["L0_attn_wv"], f32))
    wq = np.asarray(inputs["L0_attn_wq"], f32)
    pre["w_qh"] = np.ascontiguousarray(wq.reshape(D, NH, HD).transpose(1, 0, 2))
    pre["wo"] = np.ascontiguousarray(np.asarray(inputs["L0_attn_wo"], f32))
    pre["wcat"] = np.ascontiguousarray(np.concatenate(
        [np.asarray(inputs["L0_ffn_W_res"], f32).T,
         np.asarray(inputs["L0_ffn_W_pre"], f32).T,
         np.asarray(inputs["L0_ffn_W_post"], f32).T], axis=1))
    pre["basecat"] = np.ascontiguousarray(np.concatenate(
        [np.asarray(inputs["L0_ffn_H_res_base"], f32).reshape(16),
         np.asarray(inputs["L0_ffn_H_pre_base"], f32).reshape(4),
         np.asarray(inputs["L0_ffn_H_post_base"], f32).reshape(4)]))[None, :]
    pre["gate"] = np.asarray(inputs["L0_ffn_gate"], f32).reshape(1, 1)
    pre["wbr"] = np.ascontiguousarray(np.asarray(inputs["L0_ffn_W_branch"], f32))

    pre["wqT1"] = np.ascontiguousarray(np.asarray(inputs["L1_attn_wq"], f32).T)
    pre["wkT1"] = np.ascontiguousarray(np.asarray(inputs["L1_attn_wk"], f32).T)
    pre["wvT1"] = np.ascontiguousarray(np.asarray(inputs["L1_attn_wv"], f32).T)
    pre["wq1h"] = np.ascontiguousarray(pre["wqT1"].reshape(D, 32, 128).transpose(1, 0, 2))
    pre["wk1h"] = np.ascontiguousarray(pre["wkT1"].reshape(D, 32, 128).transpose(1, 0, 2))
    pre["wv1h"] = np.ascontiguousarray(pre["wvT1"].reshape(D, 32, 128).transpose(1, 0, 2))
    pre["bq1"] = np.ascontiguousarray(np.asarray(inputs["L1_attn_bq"], f32).reshape(D, 1))
    pre["bk1"] = np.ascontiguousarray(np.asarray(inputs["L1_attn_bk"], f32).reshape(D, 1))
    pre["bv1"] = np.ascontiguousarray(np.asarray(inputs["L1_attn_bv"], f32).reshape(1, D))

    pre["wTq1"] = np.ascontiguousarray(np.asarray(inputs["L1_ffn_weight_q"], np.int32).T)
    scale = np.asarray(inputs["L1_ffn_scale"], f32)
    zp = np.asarray(inputs["L1_ffn_zp"], f32)
    pre["spair"] = np.ascontiguousarray(scale.T.reshape(32, 2, D).transpose(1, 0, 2))
    pre["sexp"] = np.ascontiguousarray(np.repeat(scale.T, 64, axis=0))
    pre["scaleT"] = np.ascontiguousarray(scale.T)
    pre["zpT"] = np.ascontiguousarray(zp.T)
    pre["bias1"] = np.ascontiguousarray(np.asarray(inputs["L1_ffn_bias"], f32).reshape(D, 1))
    pre["ident"] = np.eye(128, dtype=f32)
    g2 = np.zeros((2, 128), dtype=f32)
    g2[0, 0:64] = 1.0
    g2[1, 64:128] = 1.0
    pre["g2in"] = g2
    pre["padsc"] = [np.zeros((1, 2), dtype=f32) for c in range(NCORE)]
    bexps = []
    for c in range(NCORE):
        nA, nB = 4 * c, 60 - 4 * c
        b = np.full(NSLOT, -115.0, dtype=f32)
        b[0:nA] = -EXPC0
        b[LA:LA + nB] = -EXPC0
        b[LA + LB:] = -EXPC0
        bexps.append(np.ascontiguousarray(np.broadcast_to(b[None, :], (128, NSLOT)).copy()))
    pre["bexp"] = bexps
    return pre


def assemble_kv(k_own_list, v_own_list):
    import ml_dtypes
    bfd = ml_dtypes.bfloat16
    k_glob = np.zeros((HD, T), dtype=bfd)
    v_glob = np.zeros((T, HD), dtype=bfd)
    for c in range(NCORE):
        ko = np.asarray(k_own_list[c])
        vo = np.asarray(v_own_list[c])
        vtok = vo.reshape(128, NT, HD).transpose(1, 0, 2).reshape(TPC, HD)
        for li, qt in enumerate((c, 15 - c)):
            k_glob[:, qt * 512:(qt + 1) * 512] = ko[:, li * 512:(li + 1) * 512]
            v_glob[qt * 512:(qt + 1) * 512] = vtok[li * 512:(li + 1) * 512]
    k_all, v_all = [], []
    for c in range(NCORE):
        nA, nB = 4 * c, 60 - 4 * c
        k = np.zeros((HD, NSLOT * 128), dtype=bfd)
        v = np.zeros((128, NSLOT * 128), dtype=bfd)
        k[:, 0:nA * 128] = k_glob[:, 0:nA * 128]
        k[:, LA * 128:(LA + nB) * 128] = k_glob[:, 0:nB * 128]
        qa, qb = c, 15 - c
        k[:, (LA + LB) * 128:(LA + LB + 4) * 128] = k_glob[:, qa * 512:(qa + 1) * 512]
        k[:, (LA + LB + 4) * 128:] = k_glob[:, qb * 512:(qb + 1) * 512]

        def vslot(dst_slot, gchunk):
            v[:, dst_slot * 128:(dst_slot + 1) * 128] = \
                v_glob[gchunk * 128:(gchunk + 1) * 128, :]
        for j in range(nA):
            vslot(j, j)
        for j in range(nB):
            vslot(LA + j, j)
        for j in range(4):
            vslot(LA + LB + j, qa * 4 + j)
            vslot(LA + LB + 4 + j, qb * 4 + j)
        k_all.append(np.ascontiguousarray(k))
        v_all.append(np.ascontiguousarray(v))
    return k_all, v_all


def emit_program(group):
    ph = set(group)
    nc = bass.Bass("TRN2", target_bir_lowering=False,
                   detect_race_conditions=RACE_DETECT)
    es = contextlib.ExitStack()
    es.__enter__()
    p = Prog()
    ins = {}
    outs = {}

    def din(name, shape, dtype=F32):
        t = nc.dram_tensor(name, list(shape), dtype, kind="ExternalInput").ap()
        ins[name] = t
        return t

    def dout(name, shape, dtype=F32):
        t = nc.dram_tensor(name, list(shape), dtype, kind="ExternalOutput").ap()
        outs[name] = t
        return t

    def dint(name, shape, dtype=F32):
        return nc.dram_tensor(name, list(shape), dtype, kind="Internal").ap()

    def sbuf(name, shape, dtype):
        return es.enter_context(nc.sbuf_tensor(name, list(shape), dtype))

    def psum(name, shape, dtype=F32):
        return es.enter_context(nc.psum_tensor(name, list(shape), dtype))

    def handoff(name, shape, dtype, produced, consumed):
        prod = any(q in ph for q in produced)
        cons = any(q in ph for q in consumed)
        if prod and cons:
            return dint(name + "_i", shape, dtype), "int"
        if prod:
            return dout(name, shape, dtype), "out"
        if cons:
            return din(name, shape, dtype), "in"
        return None, "none"

    if ph & {"p1", "p2"}:
        x_own = din("x_own", (TPC, D))
    if "p1" in ph:
        wk_d = din("wk", (D, HD))
        wv_d = din("wv", (D, HD))
    if "p2" in ph:
        wcat_d = din("wcat", (D, 24))
        base_d = din("basecat", (1, 24))
        gate_d = din("gate", (1, 1))
        wbr_d = din("wbr", (1024, 1024))
    if "p3a" in ph:
        wqh_d = din("w_qh", (NH, D, HD))
    FUSED_KV = ("p1" in ph) and ("p3b" in ph)
    if "p3b" in ph:
        wo_d = din("wo", (D, D))
        padsc_d = din("padsc", (1, 2))
        bexp_d = din("bexp", (128, NSLOT))
        if DEBUG_P3B:
            dbg_d = dout("dbg", (128, 2048))
        if FUSED_KV:
            k_ag_d = dint("k_ag", (NCORE * HD, TPC), BF16)
            v_ag_d = dint("v_ag", (NCORE * 128, NT * HD), BF16)
        else:
            k_all_d = din("k_all", (HD, NSLOT * 128), BF16)
            v_all_d = din("v_all", (128, NSLOT * 128), BF16)
    ident_d = din("ident", (128, 128))
    if "p5" in ph:
        wq1h_d = din("wq1h", (32, D, 128))
        wk1h_d = din("wk1h", (32, D, 128))
        wv1h_d = din("wv1h", (32, D, 128))
        bq1_d = din("bq1", (D, 1))
        bk1_d = din("bk1", (D, 1))
        bv1_d = din("bv1", (1, D))
        wTq1_d = din("wTq1", (D, D), I32)
        sexp_d = din("sexp", (D, D))
        scaleT_d = din("scaleT", (64, D))
        zpT_d = din("zpT", (64, D))
        bias1_d = din("bias1", (D, 1))
        g2_d = din("g2in", (2, 128))
        out_d = dout("out", (D, TPC))
        mk5 = dout if DEBUG_P5 else dint
        if DEBUG_P5:
            dbg5_d = dout("dbg5", (128, 4 * 8192))
            dbg6_d = dout("dbg6", (128, 6144), BF16)
        q1_d = mk5("q1_scr", (D, TPC), BF16)
        k1_d = mk5("k1_scr", (D, TPC), BF16)
        v1t_d = mk5("v1t_scr", (128, NT * D), BF16)

    xn_io, xn_mode = handoff("xn_h", (D, TPC), BF16, ["p1"], ["p2", "p3a"])
    var_io, var_mode = handoff("var_h", (128, NT), F32, ["p1"], ["p2"])
    if "p1" in ph:
        if "p3b" in ph:
            k_own_io = dint("k_own_i", (HD, TPC), BF16)
            v_own_io = dint("v_own_i", (128, NT * HD), BF16)
        else:
            k_own_io = dout("k_own", (HD, TPC), BF16)
            v_own_io = dout("v_own", (128, NT * HD), BF16)
    xf_io, xf_mode = handoff("xf_h", (128, NT * D), BF16, ["p2"], ["p3b"])
    q_io, q_mode = handoff("q0_h", (D, TPC), BF16, ["p3a"], ["p3b"])
    x1_io, x1_mode = handoff("x1_h", (128, NT * D), BF16, ["p3b"], ["p4"])
    xn1_io, xn1_mode = handoff("xn1_h", (D, TPC), BF16, ["p4"], ["p5"])
    x1f_io, x1f_mode = handoff("x1f_h", (D, TPC), BF16, ["p4"], ["p5"])

    xbuf = sbuf("xbuf", (128, NT * D), BF16)
    nbuf = sbuf("nbuf", (128, NT * D), BF16)
    scrA = sbuf("scrA", (128, 16384), BF16)
    scrB = sbuf("scrB", (128, 16384), BF16)
    stat = sbuf("stat", (128, 32), F32)
    ident_sb = sbuf("ident_sb", (128, 128), BF16)
    ones_rowf = sbuf("ones_rowf", (1, 128), F32)
    ones_colb = sbuf("ones_colb", (128, 1), BF16)
    cexp = sbuf("cexp", (128, 4), F32)
    misc = sbuf("misc", (128, 64), F32)
    rowf = sbuf("rowf", (1, 512), F32)
    if ph & {"p1", "p2", "p4"}:
        vsave = sbuf("vsave", (128, NT), F32)
    if "p2" in ph:
        lgbuf = sbuf("lgbuf", (128, 192), F32)
    if "p3b" in ph:
        pass
    if "p5" in ph:
        ones_rowb = sbuf("ones_rowb", (1, 512), BF16)
        bcol = sbuf("bcol", (128, 96), F32)
        ew = sbuf("ew", (128, 192), BF16)
        g2 = sbuf("g2", (2, 128), BF16)
        brow = sbuf("brow", (1, 256), BF16)

    BK = [psum(f"pbk{i}", (128, 512), F32) for i in range(8)]

    def bkbf(i):
        return BK[i][:].bitcast(BF16)

    def nb(c, t0, n):
        return nbuf[:, c * TPC + t0:c * TPC + t0 + n]

    def xbt(i, f0, n):
        return xbuf[:, i * D + f0:i * D + f0 + n]

    def fv(tensor, off_f32, pdim, n):
        return tensor[0:pdim, 2 * off_f32:2 * (off_f32 + n)].bitcast(F32)

    p.v.memset(ones_rowf[:], 1.0)
    if "p5" in ph:
        p.v.memset(ones_rowb[:], 1.0)
    p.v.memset(ones_colb[:], 1.0)
    p.v.memset(cexp[:, 0:1], -EXPC0)
    p.v.memset(cexp[:, 1:2], -EXPC1)
    p.v.memset(cexp[:, 2:3], EPS_LN)
    p.v.memset(cexp[:, 3:4], EPS_RMS).inc("Vs")

    _EM = {"gpsimd": p.g, "sync": p.y, "scalar": p.s, "vector": p.v, "tensor": p.t}

    def dma(eng, out, in_, sem, **kw):
        h = _EM[eng].dma_start(out=out, in_=in_, **kw)
        h.inc(sem, 16)
        ri, in_sp, out_sp = p.pending_dma[eng]
        cnt = p.count[sem]
        for sp in ([out_sp] if out_sp else []):
            tn, lo, hi = sp
            keep = [e for e in p.lastw[tn] if not (e[0] < hi and lo < e[1])]
            keep.append((lo, hi, sem, cnt, eng))
            p.lastw[tn] = keep[-4096:]
            p.lastr[tn] = [e for e in p.lastr[tn]
                           if not (e[0] < hi and lo < e[1])]
        for sp in ([in_sp] if in_sp else []):
            tn, lo, hi = sp
            lst = p.lastr[tn]
            for i in range(len(lst) - 1, -1, -1):
                e = lst[i]
                if e[0] == lo and e[1] == hi and e[2] == sem and e[4] == eng:
                    del lst[i]
            lst.append((lo, hi, sem, cnt, eng))
        return cnt

    dma("gpsimd", ident_sb[:], ident_d, "Dg")

    def stats_ln(src_ap, sq_scr, save_var_col=None):
        mu = stat[:, 0:1]
        s1 = stat[:, 1:2]
        s2 = stat[:, 2:3]
        ex2 = stat[:, 3:4]
        var = stat[:, 4:5]
        rstd = stat[:, 5:6]
        p.v.reduce_sum(s1, src_ap, axis=AX.X).inc("Vs")
        p.s.activation(sq_scr, src_ap, AF.Square, accum_out=s2).inc("Ss")
        p.wait("vector", "Ss")
        p.v.tensor_scalar_mul(mu, s1, 1.0 / D)
        p.v.tensor_scalar_mul(ex2, s2, 1.0 / D)
        p.v.tensor_tensor(var, mu, mu, ALU.mult)
        p.v.tensor_tensor(var, ex2, var, ALU.subtract)
        if save_var_col is not None:
            p.v.tensor_copy(save_var_col, var)
        p.lastinc("vector", "Vs")
        p.wait("scalar", "Vs")
        p.s.activation(rstd, var, AF.Ln, bias=cexp[:, 2:3])
        p.s.activation(rstd, rstd, AF.Exp, scale=-0.5).inc("Ss")
        p.wait("vector", "Ss")
        return mu, rstd

    def transpose_32(src_tile_fn, dst_fn):
        for grpi in range(8):
            bk = 6 + (grpi % 2)
            c0 = grpi * 4
            p.wait("tensor", "Vs")
            for j in range(4):
                h = p.t.transpose(bkbf(bk)[:, j * 128:(j + 1) * 128],
                                  src_tile_fn(c0 + j), ident_sb[:])
            h.inc("Ts")
            p.wait("vector", "Ts")
            src = bkbf(bk)[:, 0:512].rearrange("p (a b) -> p a b", b=128)
            p.v.tensor_copy(dst_fn(c0), src).inc("Vs")

    def export_fm(dst_dram, sem):
        for q4 in range(4):
            src = nbuf[:].rearrange("p (c t) -> p c t", t=TPC)[:, q4 * 8:(q4 + 1) * 8, :]
            dst = dst_dram.rearrange("(c p) t -> p c t", p=128)[:, q4 * 8:(q4 + 1) * 8, :]
            dma("sync", dst, src, sem)

    def import_fm(src_dram, sem):
        for q4 in range(4):
            dst = nbuf[:].rearrange("p (c t) -> p c t", t=TPC)[:, q4 * 8:(q4 + 1) * 8, :]
            src = src_dram.rearrange("(c p) t -> p c t", p=128)[:, q4 * 8:(q4 + 1) * 8, :]
            dma("sync", dst, src, sem)

    if "p1" in ph:
        for q4 in range(4):
            src = wk_d.rearrange("(c p) d -> p c d", p=128)[:, q4 * 8:(q4 + 1) * 8, :]
            dst = scrB[:, 0:4096].rearrange("p (c d) -> p c d", d=128)[:, q4 * 8:(q4 + 1) * 8, :]
            dma("gpsimd", dst, src, "Dg")
            src = wv_d.rearrange("(c p) d -> p c d", p=128)[:, q4 * 8:(q4 + 1) * 8, :]
            dst = scrB[:, 4096:8192].rearrange("p (c d) -> p c d", d=128)[:, q4 * 8:(q4 + 1) * 8, :]
            dma("gpsimd", dst, src, "Dg")
        wtok = p.snap("Dg")
        k_own_sb = scrA[:, 0:TPC]
        v_own_sb = scrA[:, TPC:2 * TPC]
        sqs = scrA[:, 2048:6144]
        vstage = scrA[:, 6144:6272]
        xrot = [scrB[:, 8192 + i * D:8192 + (i + 1) * D] for i in range(2)]
        for i in range(NT):
            xt = xrot[i % 2]
            p.wait("gpsimd", "Ts")
            p.wait("gpsimd", "Vs")
            p.wait("gpsimd", "Ss")
            tok = dma("gpsimd", xt, x_own[i * 128:(i + 1) * 128, :], "Dg")
            p.wait("vector", "Dg", tok)
            p.wait("scalar", "Dg", tok)
            mu, rstd = stats_ln(xt, sqs, vsave[:, i:i + 1])
            p.v.tensor_scalar(xt, xt, mu, rstd, ALU.subtract, ALU.mult).inc("Vs")
            p.wait("tensor", "Vs")
            p.wait("tensor", "Dg", wtok)
            transpose_32(
                lambda c: xt[:, c * 128:(c + 1) * 128],
                lambda c0: nbuf[:].rearrange("p (c t) -> p c t", t=TPC)[
                    :, c0:c0 + 4, i * 128:(i + 1) * 128])
            p.wait("tensor", "Vs")
            for c in range(NKC):
                hh = p.t.matmul(BK[0][:, 0:128], scrB[:, c * 128:(c + 1) * 128],
                                nb(c, i * 128, 128), start=(c == 0), stop=(c == NKC - 1))
            hh.inc("Ts")
            for c in range(NKC):
                hh = p.t.matmul(BK[1][:, 0:128], scrB[:, 4096 + c * 128:4096 + (c + 1) * 128],
                                nb(c, i * 128, 128), start=(c == 0), stop=(c == NKC - 1))
            hh.inc("Ts")
            p.wait("vector", "Ts")
            p.v.tensor_copy(k_own_sb[:, i * 128:(i + 1) * 128], BK[0][:, 0:128])
            p.v.tensor_copy(vstage, BK[1][:, 0:128])
            p.lastinc("vector", "Vs")
            p.wait("tensor", "Vs")
            p.t.transpose(bkbf(2)[:, 0:128], vstage, ident_sb[:]).inc("Ts")
            p.wait("vector", "Ts")
            p.v.tensor_copy(v_own_sb[:, i * 128:(i + 1) * 128], bkbf(2)[:, 0:128]).inc("Vs")
        p.wait("sync", "Vs")
        dma("sync", k_own_io, k_own_sb, "Dy")
        dma("sync", v_own_io, v_own_sb, "Dy")
        if xn_mode == "out":
            export_fm(xn_io, "Dy")
            dma("sync", var_io, vsave[:], "Dy")

    if "p2" in ph:
        if xn_mode == "in":
            import_fm(xn_io, "Dy")
            dma("sync", vsave[:], var_io, "Dy")
        for i in range(NT):
            dma("gpsimd", xbuf[:, i * D:(i + 1) * D], x_own[i * 128:(i + 1) * 128, :], "Dg")
        for c in range(NKC):
            dma("gpsimd", scrB[:, c * 24:(c + 1) * 24], wcat_d[c * 128:(c + 1) * 128, :], "Dg")
        dma("sync", rowf[0:1, 0:24], base_d, "Dy")
        dma("sync", misc[0:1, 30:31], gate_d, "Dy")
        p.wait("tensor", "Dy")
        p.wait("tensor", "Vs")
        p.t.matmul(BK[0][:, 0:24], ones_rowf[0:1, 0:128], rowf[0:1, 0:24],
                   start=True, stop=True).inc("Ts")
        p.wait("scalar", "Dy")
        p.s.activation(misc[0:1, 31:32], misc[0:1, 30:31], AF.Sigmoid).inc("Ss")
        p.wait("tensor", "Ss")
        p.t.matmul(BK[1][:, 0:1], ones_rowf[0:1, 0:128], misc[0:1, 31:32],
                   start=True, stop=True).inc("Ts")
        p.wait("vector", "Ts")
        p.v.tensor_copy(misc[:, 0:24], BK[0][:, 0:24])
        p.v.tensor_scalar_mul(misc[:, 0:16], misc[:, 0:16], 10.0)
        p.v.tensor_copy(misc[:, 32:33], BK[1][:, 0:1])
        p.v.tensor_scalar(misc[:, 33:34], misc[:, 32:33], -1.0, 1.0,
                          ALU.mult, ALU.add).inc("Vs")
        p.wait("vector", "Dy")
        for i in range(NT):
            t0 = stat[:, 6:7]
            p.v.tensor_scalar_add(t0, vsave[:, i:i + 1], EPS_LN)
            p.v.reciprocal(t0, t0)
            p.v.tensor_tensor(t0, vsave[:, i:i + 1], t0, ALU.mult).inc("Vs")
            p.wait("scalar", "Vs")
            p.s.activation(t0, t0, AF.Ln, bias=cexp[:, 3:4])
            p.s.activation(stat[:, 8 + i:9 + i], t0, AF.Exp, scale=-0.5).inc("Ss")
            p.wait("vector", "Ss")
        for i in range(NT):
            p.v.tensor_scalar_mul(stat[:, 16 + i:17 + i], stat[:, 8 + i:9 + i], 10.0)
        p.lastinc("vector", "Vs")
        lg = lgbuf[:]
        p.wait("tensor", "Dg")
        p.wait("tensor", "Dy")
        for i in range(NT):
            p.wait("tensor", "Vs")
            for c in range(NKC):
                hh = p.t.matmul(BK[2][:, 0:24], nb(c, i * 128, 128),
                                scrB[:, c * 24:(c + 1) * 24],
                                start=(c == 0), stop=(c == NKC - 1))
            hh.inc("Ts")
            p.wait("vector", "Ts")
            p.v.scalar_tensor_tensor(lg[:, i * 24:i * 24 + 16], BK[2][:, 0:16],
                                     stat[:, 16 + i:17 + i], misc[:, 0:16],
                                     ALU.mult, ALU.add)
            p.v.scalar_tensor_tensor(lg[:, i * 24 + 16:i * 24 + 24], BK[2][:, 16:24],
                                     stat[:, 8 + i:9 + i], misc[:, 16:24],
                                     ALU.mult, ALU.add)
            p.lastinc("vector", "Vs")
        skv = fv(scrB, 512, 128, 416)
        ub = skv[:, 0:32]
        vb = skv[:, 32:64]
        Bb = skv[:, 64:192]
        mmx = skv[:, 192:224]
        sS = skv[:, 224:256]
        lnS = skv[:, 256:288]
        Ee = skv[:, 288:416]
        p.v.memset(ub, 0.0)
        p.v.memset(vb, 0.0).inc("Vs")
        Z4 = lg.rearrange("p (s n) -> p s n", n=24)[:, :, 0:16].rearrange(
            "p s (i j) -> p s i j", j=4)
        B4 = Bb.rearrange("p (s i j) -> p s i j", i=4, j=4)
        B4ji = Bb.rearrange("p (s i j) -> p s j i", i=4, j=4)
        E4 = Ee.rearrange("p (s i j) -> p s i j", i=4, j=4)
        E4ji = Ee.rearrange("p (s i j) -> p s j i", i=4, j=4)
        m3 = mmx.rearrange("p (s i) -> p s i", i=4)
        s3 = sS.rearrange("p (s i) -> p s i", i=4)

        def bc_last(ap32):
            return ap32.rearrange("p (s i) -> p s i", i=4).unsqueeze(3).broadcast_to(
                [128, 8, 4, 4])

        def bc_mid(ap32):
            return ap32.rearrange("p (s j) -> p s j", j=4).unsqueeze(2).broadcast_to(
                [128, 8, 4, 4])

        for it in range(SINK_IT):
            p.v.tensor_tensor(B4, Z4, bc_mid(vb), ALU.add)
            p.v.reduce_max(m3, B4, axis=AX.X)
            p.v.tensor_tensor(B4, B4, bc_last(mmx), ALU.subtract).inc("Vs")
            p.wait("scalar", "Vs")
            p.s.activation(Ee, Bb, AF.Exp).inc("Ss")
            p.wait("vector", "Ss")
            p.v.reduce_sum(s3, E4, axis=AX.X).inc("Vs")
            p.wait("scalar", "Vs")
            p.s.activation(lnS, sS, AF.Ln).inc("Ss")
            p.wait("vector", "Ss")
            p.v.scalar_tensor_tensor(ub, mmx, -1.0, lnS, ALU.mult, ALU.subtract)
            p.v.tensor_tensor(B4, Z4, bc_last(ub), ALU.add)
            p.v.reduce_max(m3, B4ji, axis=AX.X)
            p.v.tensor_tensor(B4, B4, bc_mid(mmx), ALU.subtract).inc("Vs")
            p.wait("scalar", "Vs")
            p.s.activation(Ee, Bb, AF.Exp).inc("Ss")
            p.wait("vector", "Ss")
            p.v.reduce_sum(s3, E4ji, axis=AX.X).inc("Vs")
            p.wait("scalar", "Vs")
            p.s.activation(lnS, sS, AF.Ln).inc("Ss")
            p.wait("vector", "Ss")
            p.v.scalar_tensor_tensor(vb, mmx, -1.0, lnS, ALU.mult, ALU.subtract)
        p.v.tensor_tensor(B4, Z4, bc_last(ub), ALU.add)
        p.v.tensor_tensor(B4, B4, bc_mid(vb), ALU.add).inc("Vs")
        p.wait("scalar", "Vs")
        Zp = lg.rearrange("p (s n) -> p s n", n=24)[:, :, 0:16]
        p.s.activation(Zp, Bb.rearrange("p (s n) -> p s n", n=16), AF.Exp).inc("Ss")
        p.wait("vector", "Ss")
        p.v.tensor_scalar(Zp, Zp, misc[:, 32:33], None, ALU.mult)
        for j in range(4):
            dg = lg.rearrange("p (s n) -> p s n", n=24)[:, :, 5 * j:5 * j + 1]
            p.v.tensor_scalar(dg, dg, misc[:, 33:34], None, ALU.add)
        for off in (16, 20):
            P3 = lg.rearrange("p (s n) -> p s n", n=24)[:, :, off:off + 4]
            mp = mmx[:, 0:8]
            mpb = mmx[:, 0:8].rearrange("p (s o) -> p s o", o=1).broadcast_to([128, 8, 4])
            p.v.reduce_max(mp, P3, axis=AX.X)
            p.v.tensor_tensor(P3, P3, mpb, ALU.subtract).inc("Vs")
            p.wait("scalar", "Vs")
            p.s.activation(P3, P3, AF.Exp).inc("Ss")
            p.wait("vector", "Ss")
            p.v.reduce_sum(mp, P3, axis=AX.X)
            p.v.reciprocal(mp, mp)
            p.v.tensor_tensor(P3, P3, mpb, ALU.mult).inc("Vs")
        xt_tok = scrB[:, 8192:12288]
        xp32 = fv(scrB, 6144, 128, 1024)
        xpb = scrB[:, 14336:15360]
        for i in range(NT):
            transpose_32(
                lambda c: nb(c, i * 128, 128),
                lambda c0: xt_tok[:, c0 * 128:(c0 + 4) * 128].rearrange(
                    "p (a b) -> p a b", b=128))
            p.v.tensor_scalar(xp32, xt_tok[:, 0:1024], lg[:, i * 24 + 16:i * 24 + 17],
                              None, ALU.mult)
            for n in range(1, 4):
                p.v.scalar_tensor_tensor(xp32, xt_tok[:, n * 1024:(n + 1) * 1024],
                                         lg[:, i * 24 + 16 + n:i * 24 + 17 + n], xp32,
                                         ALU.mult, ALU.add)
            p.v.tensor_copy(xpb, xp32).inc("Vs")
            p.wait("tensor", "Vs")
            for grpi in range(2):
                bk = 6 + grpi
                c0 = grpi * 4
                for j in range(4):
                    h = p.t.transpose(bkbf(bk)[:, j * 128:(j + 1) * 128],
                                      xpb[:, (c0 + j) * 128:(c0 + j + 1) * 128],
                                      ident_sb[:])
                h.inc("Ts")
                p.wait("vector", "Ts")
                dst = scrA[:, 0:8192].rearrange("p (c t) -> p c t", t=1024)[
                    :, c0:c0 + 4, i * 128:(i + 1) * 128]
                p.v.tensor_copy(dst, bkbf(bk)[:, 0:512].rearrange(
                    "p (a b) -> p a b", b=128)).inc("Vs")
        p.wait("gpsimd", "Ts")
        dma("gpsimd", scrB[:, 0:8192].rearrange("p (c f) -> p c f", f=1024),
            wbr_d.rearrange("(c p) f -> p c f", p=128), "Dg")
        wtok = p.snap("Dg")
        p.wait("tensor", "Dg", wtok)
        for fc in range(8):
            for th in range(2):
                p.wait("tensor", "Vs")
                for cc in range(8):
                    hh = p.t.matmul(
                        BK[2][:],
                        scrB[:, cc * 1024 + fc * 128:cc * 1024 + (fc + 1) * 128],
                        scrA[:, cc * 1024 + th * 512:cc * 1024 + (th + 1) * 512],
                        start=(cc == 0), stop=(cc == 7))
                hh.inc("Ts")
                p.wait("vector", "Ts")
                p.v.tensor_copy(scrA[:, 8192 + fc * 1024 + th * 512:
                                     8192 + fc * 1024 + (th + 1) * 512],
                                BK[2][:]).inc("Vs")
        ytk = scrB[:, 15360:16384]
        acc = fv(scrA, 0, 128, 4096)
        for i in range(NT):
            transpose_32(
                lambda c: nb(c, i * 128, 128),
                lambda c0: xt_tok[:, c0 * 128:(c0 + 4) * 128].rearrange(
                    "p (a b) -> p a b", b=128))
            p.wait("tensor", "Vs")
            for grpi in range(2):
                bk = 6 + grpi
                c0 = grpi * 4
                for j in range(4):
                    h = p.t.transpose(
                        bkbf(bk)[:, j * 128:(j + 1) * 128],
                        scrA[:, 8192 + (c0 + j) * 1024 + i * 128:
                             8192 + (c0 + j) * 1024 + (i + 1) * 128],
                        ident_sb[:])
                h.inc("Ts")
                p.wait("vector", "Ts")
                p.v.tensor_copy(ytk[:, c0 * 128:(c0 + 4) * 128].rearrange(
                    "p (a b) -> p a b", b=128), bkbf(bk)[:, 0:512].rearrange(
                    "p (a b) -> p a b", b=128)).inc("Vs")
            for ii in range(4):
                sl = acc[:, ii * 1024:(ii + 1) * 1024]
                p.v.tensor_scalar(sl, xt_tok[:, 0:1024],
                                  lg[:, i * 24 + ii * 4:i * 24 + ii * 4 + 1],
                                  None, ALU.mult)
                for j in range(1, 4):
                    p.v.scalar_tensor_tensor(
                        sl, xt_tok[:, j * 1024:(j + 1) * 1024],
                        lg[:, i * 24 + ii * 4 + j:i * 24 + ii * 4 + j + 1],
                        sl, ALU.mult, ALU.add)
                p.v.scalar_tensor_tensor(sl, ytk,
                                         lg[:, i * 24 + 20 + ii:i * 24 + 21 + ii],
                                         sl, ALU.mult, ALU.add)
            p.wait("vector", "Dg")
            p.v.tensor_tensor(xbuf[:, i * D:(i + 1) * D], acc,
                              xbuf[:, i * D:(i + 1) * D], ALU.add).inc("Vs")
        if xf_mode == "out":
            p.wait("sync", "Vs")
            dma("sync", xf_io, xbuf[:], "Dy")

    if "p3a" in ph:
        if xn_mode == "in" and "p2" not in ph:
            import_fm(xn_io, "Dy")
        for h in range(NH):
            r = h % 2
            wq_sb = scrA[:, r * 4096:(r + 1) * 4096]
            for q4 in range(4):
                src = wqh_d[h].rearrange("(c p) d -> p c d", p=128)[
                    :, q4 * 8:(q4 + 1) * 8, :]
                wtok = dma("gpsimd", wq_sb.rearrange("p (c d) -> p c d", d=128)[
                    :, q4 * 8:(q4 + 1) * 8, :], src, "Dg")
            p.wait("tensor", "Dg", wtok)
            for c in range(NKC):
                for qt in range(2):
                    hh = p.t.matmul(BK[2 * r + qt][:],
                                    wq_sb[:, c * 128:(c + 1) * 128],
                                    nb(c, qt * 512, 512),
                                    start=(c == 0), stop=(c == NKC - 1),
                                    skip_group_check=True)
            q_sb = scrB[:, r * 1024:(r + 1) * 1024]
            for qt in range(2):
                p.v.tensor_copy(q_sb[:, qt * 512:(qt + 1) * 512], BK[2 * r + qt][:])
                dma("sync", q_io[h * 128:(h + 1) * 128, qt * 512:(qt + 1) * 512],
                    q_sb[:, qt * 512:(qt + 1) * 512], "Dy2")

    if "p3b" in ph:
        if xf_mode == "in":
            dma("sync", xbuf[:], xf_io, "Dy")
        dma("sync", misc[0:1, 40:42], padsc_d, "Dy")
        masks = nbuf[:, 16384:16384 + 4 * QT]
        iot = nbuf[:, 20480:21504].bitcast(I32)
        bexp_sb = fv(nbuf, 10752, 128, NSLOT)
        dma("sync", bexp_sb, bexp_d, "Dy")
        if FUSED_KV:
            p.wait("gpsimd", "Dy")
            p.g.collective_compute(
                "AllGather", ALU.bypass,
                replica_groups=[list(range(NCORE))],
                ins=[k_own_io.opt()], outs=[k_ag_d.opt()]).inc("Dcc", 1)
            p.g.collective_compute(
                "AllGather", ALU.bypass,
                replica_groups=[list(range(NCORE))],
                ins=[v_own_io.opt()], outs=[v_ag_d.opt()]).inc("Dcc", 1)
            p.wait("sync", "Dcc")
            for q in range(16):
                r = q if q < 8 else 15 - q
                off = 0 if q < 8 else 512
                ksrc = k_ag_d[r * 128:(r + 1) * 128, off:off + 512]
                vsrc = v_ag_d[r * 128:(r + 1) * 128, off:off + 512]
                if q < 8:
                    dma("sync", scrA[:, q * 512:(q + 1) * 512], ksrc, "Dy")
                    dma("sync", scrB[:, q * 512:(q + 1) * 512], vsrc, "Dy")
                dma("sync", scrA[:, (LA + 4 * q) * 128:(LA + 4 * q + 4) * 128],
                    ksrc, "Dy")
                dma("sync", scrB[:, (LA + 4 * q) * 128:(LA + 4 * q + 4) * 128],
                    vsrc, "Dy")
            for dd in range(2):
                o = (LA + LB + 4 * dd) * 128
                dma("sync", scrA[:, o:o + 512],
                    k_own_io[:, dd * 512:(dd + 1) * 512], "Dy")
                dma("sync", scrB[:, o:o + 512],
                    v_own_io[:, dd * 512:(dd + 1) * 512], "Dy")
        else:
            for q4 in range(4):
                n4 = NSLOT * 128 // 4
                dma("sync", scrA[:, q4 * n4:(q4 + 1) * n4],
                    k_all_d[:, q4 * n4:(q4 + 1) * n4], "Dy")
                dma("sync", scrB[:, q4 * n4:(q4 + 1) * n4],
                    v_all_d[:, q4 * n4:(q4 + 1) * n4], "Dy")
        kvtok = p.snap("Dy")
        for dd in range(4):
            p.wait("gpsimd", "Vs")
            p.g.iota(iot, [[1, QT]], base=-dd * 128, channel_multiplier=-1).inc("Gs")
            p.wait("vector", "Gs")
            p.v.tensor_scalar(masks[:, dd * QT:(dd + 1) * QT], iot, 0, None,
                              ALU.is_ge).inc("Vs")
        attnout = nbuf[:, 0:16384]
        qrot = [scrA[:, 13312 + r * 512:13312 + (r + 1) * 512] for r in range(2)]
        SCB = [0, 1, 5, 6]
        prot = [nbuf[:, 22528 + r * 512:22528 + (r + 1) * 512] for r in range(4)]
        reprot = [fv(scrB, 7168 + r * 512, 128, 512) for r in range(2)]
        tmpb = scrA[:, 15360:16384]
        p.wait("tensor", "Dy", kvtok)
        p.wait("vector", "Dy", kvtok)
        for qt in range(2):
            L = LA if qt == 0 else LB
            regoff = 0 if qt == 0 else LA
            diagoff = LA + LB + 4 * qt
            for h in range(NH):
                r = h % 2
                p.wait("sync", "Ts")
                qtok = dma("sync", qrot[r],
                           q_io[h * 128:(h + 1) * 128, qt * 512:(qt + 1) * 512], "Dq")
                p.wait("tensor", "Dq", qtok)
                def slot_of(u):
                    return (regoff + u) if u < L else (diagoff + (u - L))
                for uu in range(L + 6):
                    if uu < L + 4:
                        u = uu
                        slot = slot_of(u)
                        sb = BK[SCB[u % 4]]
                        pb = prot[u % 4]
                        p.t.matmul(sb[:], scrA[:, slot * 128:(slot + 1) * 128],
                                   qrot[r], start=True, stop=True,
                                   skip_group_check=True)
                        p.s.activation(pb, sb[:], AF.Exp, scale=SCALE_ATT,
                                       bias=bexp_sb[:, slot:slot + 1])
                        if u >= L:
                            p.v.tensor_tensor(pb, pb,
                                              masks[:, (u - L) * QT:(u - L + 1) * QT],
                                              ALU.mult)
                    if uu >= 2:
                        u = uu - 2
                        slot = slot_of(u)
                        pb = prot[u % 4]
                        p.t.matmul(BK[2][:], scrB[:, slot * 128:(slot + 1) * 128], pb,
                                   start=(u == 0), stop=(u == L + 3),
                                   skip_group_check=True)
                        hh = p.t.matmul(BK[3][0:1, :], ones_colb[:], pb,
                                        start=(u == 0), stop=(u == L + 3),
                                        skip_group_check=True)
                hh.inc("Ts")
                p.wait("vector", "Ts")
                p.v.tensor_copy(rowf[0:1, 0:512], BK[3][0:1, :])
                if DEBUG_P3B and qt == 0 and h == 0:
                    p.v.tensor_copy(fv(nbuf, 8192, 1, 512), rowf[0:1, 0:512]).inc("Vs")
                p.v.tensor_scalar(rowf[0:1, 0:512], rowf[0:1, 0:512],
                                  misc[0:1, 40 + qt:41 + qt], None, ALU.subtract)
                p.v.reciprocal(rowf[0:1, 0:512], rowf[0:1, 0:512]).inc("Vs")
                if DEBUG_P3B and qt == 0 and h == 0:
                    p.v.tensor_copy(fv(nbuf, 8704, 1, 512), rowf[0:1, 0:512])
                    p.v.tensor_copy(fv(nbuf, 9216, 128, 512), BK[2][:]).inc("Vs")
                p.wait("tensor", "Vs")
                p.t.matmul(BK[4][:], ones_rowf[0:1, 0:128], rowf[0:1, 0:512],
                           start=True, stop=True, skip_group_check=True).inc("Ts")
                p.wait("vector", "Ts")
                p.v.tensor_copy(reprot[r], BK[4][:])
                p.v.tensor_tensor(attnout[:, h * 512:(h + 1) * 512], BK[2][:],
                                  reprot[r], ALU.mult)
                p.lastinc("vector", "Vs")
            worot = [scrA[:, 13312 + r2 * 1024:13312 + (r2 + 1) * 1024]
                     for r2 in range(2)]
            for fg in range(4):
                for h in range(NH):
                    r2 = h % 2
                    p.wait("gpsimd", "Ts")
                    wtok = dma("gpsimd", worot[r2],
                               wo_d[h * 128:(h + 1) * 128, fg * 1024:(fg + 1) * 1024],
                               "Dg")
                    p.wait("tensor", "Dg", wtok)
                    if h < 2:
                        p.wait("tensor", "Vs")
                    for fs in range(8):
                        hh = p.t.matmul(BK[fs][:], worot[r2][:, fs * 128:(fs + 1) * 128],
                                        attnout[:, h * 512:(h + 1) * 512],
                                        start=(h == 0), stop=(h == NH - 1),
                                        skip_group_check=True)
                        if fs == 7:
                            hh.inc("Ts")
                p.wait("vector", "Ts")
                for fs in range(8):
                    p.v.tensor_copy(tmpb[:, 0:512], BK[fs][:]).inc("Vs")
                    p.wait("tensor", "Vs")
                    for qs in range(4):
                        h2 = p.t.transpose(bkbf(fs)[:, qs * 128:(qs + 1) * 128],
                                           tmpb[:, qs * 128:(qs + 1) * 128], ident_sb[:])
                    h2.inc("Ts")
                    p.wait("vector", "Ts")
                    for qs in range(4):
                        sl = xbt(qt * 4 + qs, fg * 1024 + fs * 128, 128)
                        p.v.scalar_tensor_tensor(sl, bkbf(fs)[:, qs * 128:(qs + 1) * 128],
                                                 1.0, sl, ALU.mult, ALU.add)
                    p.lastinc("vector", "Vs")
        if DEBUG_P3B:
            p.wait("sync", "Vs")
            dma("sync", dbg_d[:, 0:2048], fv(nbuf, 8192, 128, 2048), "Dy")
        if x1_mode == "out":
            p.wait("sync", "Vs")
            dma("sync", x1_io, xbuf[:], "Dy")

    if "p4" in ph:
        if x1_mode == "in":
            dma("sync", xbuf[:], x1_io, "Dy")
            p.wait("vector", "Dy")
            p.wait("scalar", "Dy")
            p.wait("tensor", "Dy")
        sqs = scrB[:, 0:4096]
        xn1t = scrB[:, 4096:8192]
        x1st = scrB[:, 8192:8704]
        for i in range(NT):
            mu, rstd = stats_ln(xbuf[:, i * D:(i + 1) * D], sqs, vsave[:, i:i + 1])
            p.wait("vector", "Ts")
            p.v.tensor_scalar(xn1t, xbuf[:, i * D:(i + 1) * D], mu, rstd,
                              ALU.subtract, ALU.mult).inc("Vs")
            p.wait("tensor", "Vs")
            transpose_32(
                lambda c: xn1t[:, c * 128:(c + 1) * 128],
                lambda c0: nbuf[:].rearrange("p (c t) -> p c t", t=TPC)[
                    :, c0:c0 + 4, i * 128:(i + 1) * 128])
            for grpi in range(8):
                bk = 6 + (grpi % 2)
                c0 = grpi * 4
                p.wait("tensor", "Vs")
                for j in range(4):
                    h = p.t.transpose(
                        bkbf(bk)[:, j * 128:(j + 1) * 128],
                        xbuf[:, i * D + (c0 + j) * 128:i * D + (c0 + j + 1) * 128],
                        ident_sb[:])
                h.inc("Ts")
                p.wait("vector", "Ts")
                p.wait("vector", "Dy2")
                p.v.tensor_copy(x1st[:], bkbf(bk)[:, 0:512]).inc("Vs")
                p.wait("sync", "Vs")
                dst = x1f_io.rearrange("(c p) t -> p c t", p=128)[
                    :, c0:c0 + 4, i * 128:(i + 1) * 128]
                dma("sync", dst, x1st[:].rearrange("p (a b) -> p a b", b=128), "Dy2")
        if xn1_mode == "out":
            export_fm(xn1_io, "Dy")

    if "p5" in ph:
        if xn1_mode == "in":
            import_fm(xn1_io, "Dy")
            p.wait("tensor", "Dy")
            p.wait("vector", "Dy")
            p.wait("scalar", "Dy")
            p.wait("gpsimd", "Dy")
        sT = fv(xbuf, 0, 64, 4096)
        zT = fv(xbuf, 4096, 64, 4096)
        mT_b = xbuf[:, 16384:20480]
        dma("sync", sT, scaleT_d, "Dy")
        dma("sync", zT, zpT_d, "Dy")
        p.wait("vector", "Dy")
        p.v.tensor_tensor(sT, sT, zT, ALU.mult)
        p.v.tensor_copy(mT_b[0:64, :], sT).inc("Vs")
        p.v.memset(ew[:], 0.0)
        p.v.memset(ew[0:64, 64:65], 1.0)
        p.v.memset(ew[64:128, 65:66], 1.0).inc("Vs")
        dma("gpsimd", g2[:], g2_d, "Dg")
        for c in range(NKC):
            dma("sync", bcol[:, c:c + 1], bq1_d[c * 128:(c + 1) * 128, :], "Dy")
            dma("sync", bcol[:, 32 + c:33 + c], bk1_d[c * 128:(c + 1) * 128, :], "Dy")
            dma("sync", bcol[:, 64 + c:65 + c], bias1_d[c * 128:(c + 1) * 128, :], "Dy")
        btok = p.snap("Dy")
        DO_P = "p" in P5_PARTS
        DO_F = "f" in P5_PARTS
        DO_A = "a" in P5_PARTS
        wbig = [scrA[:, 0:12288], xbuf[:, 0:12288]]
        qst = scrB[:, 0:1024]
        kst = scrB[:, 1024:2048]
        vst = scrB[:, 2048:3072]
        vtr = scrB[:, 3072:3584]
        p.wait("vector", "Dy", btok)
        for fc in range(NKC if DO_P else 0):
            r = fc % 2
            for q4 in range(4):
                sl = slice(q4 * 8, (q4 + 1) * 8)
                src = wq1h_d[fc].rearrange("(c p) d -> p c d", p=128)[:, sl, :]
                dma("gpsimd", wbig[r][:, 0:4096].rearrange(
                    "p (c d) -> p c d", d=128)[:, sl, :], src, "Dg4")
                src = wk1h_d[fc].rearrange("(c p) d -> p c d", p=128)[:, sl, :]
                dma("gpsimd", wbig[r][:, 4096:8192].rearrange(
                    "p (c d) -> p c d", d=128)[:, sl, :], src, "Dg4")
                src = wv1h_d[fc].rearrange("(c p) d -> p c d", p=128)[:, sl, :]
                wtok = dma("gpsimd", wbig[r][:, 8192:12288].rearrange(
                    "p (c d) -> p c d", d=128)[:, sl, :], src, "Dg4")
            p.wait("tensor", "Dg4", wtok)
            for c in range(NKC):
                for th in range(2):
                    rhs = nb(c, th * 512, 512)
                    p.t.matmul(BK[0 + th][:], wbig[r][:, c * 128:(c + 1) * 128], rhs,
                               start=(c == 0), stop=(c == NKC - 1), skip_group_check=True)
                    p.t.matmul(BK[2 + th][:], wbig[r][:, 4096 + c * 128:4096 + (c + 1) * 128],
                               rhs, start=(c == 0), stop=(c == NKC - 1),
                               skip_group_check=True)
                    hh = p.t.matmul(BK[4 + th][:], wbig[r][:, 8192 + c * 128:8192 + (c + 1) * 128],
                                    rhs, start=(c == 0), stop=False, skip_group_check=True)
                hh.inc("Ts")
            p.wait("gpsimd", "Ts")
            brtok = dma("gpsimd", brow[0:1, 0:128],
                        bv1_d[0:1, fc * 128:(fc + 1) * 128], "Dg4")
            p.wait("tensor", "Dg4", brtok)
            for th in range(2):
                hh = p.t.matmul(BK[4 + th][:], brow[0:1, 0:128], ones_rowb[0:1, 0:512],
                                start=False, stop=True, skip_group_check=True)
            hh.inc("Ts")
            p.wait("vector", "Ts")
            p.wait("vector", "Dq")
            for th in range(2):
                p.v.tensor_scalar(qst[:, th * 512:(th + 1) * 512], BK[0 + th][:],
                                  bcol[:, fc:fc + 1], None, ALU.add)
                p.v.tensor_scalar(kst[:, th * 512:(th + 1) * 512], BK[2 + th][:],
                                  bcol[:, 32 + fc:33 + fc], None, ALU.add)
                p.v.tensor_copy(vst[:, th * 512:(th + 1) * 512], BK[4 + th][:])
            p.lastinc("vector", "Vs")
            p.wait("sync", "Vs")
            dma("sync", q1_d[fc * 128:(fc + 1) * 128, :], qst, "Dq")
            dma("sync", k1_d[fc * 128:(fc + 1) * 128, :], kst, "Dq")
            for grpi in range(2):
                bk = 6 + grpi
                p.wait("tensor", "Vs")
                for j in range(4):
                    i4 = grpi * 4 + j
                    h = p.t.transpose(bkbf(bk)[:, j * 128:(j + 1) * 128],
                                      vst[:, i4 * 128:(i4 + 1) * 128], ident_sb[:])
                h.inc("Ts")
                p.wait("vector", "Ts")
                p.wait("vector", "Dq")
                p.v.tensor_copy(vtr[:], bkbf(bk)[:, 0:512]).inc("Vs")
                p.wait("sync", "Vs")
                dst = v1t_d.rearrange("p (i f) -> p i f", f=D)[
                    :, grpi * 4:(grpi + 1) * 4, fc * 128:(fc + 1) * 128]
                dma("sync", dst, vtr[:].rearrange("p (a b) -> p a b", b=128), "Dq")
        negs = xbuf[0:64, 20480:22528]
        sneg = scrB[:, 0:64]
        for ts_ in range(NT if DO_F else 0):
            p.wait("tensor", "Vs")
            for c in range(NKC):
                hh = p.t.matmul(BK[6][:, 0:64], nb(c, ts_ * 128, 128),
                                ew[:, 64 - 2 * c:128 - 2 * c],
                                start=(c == 0), stop=(c == NKC - 1),
                                skip_group_check=True)
            hh.inc("Ts")
            p.wait("vector", "Ts")
            p.v.tensor_scalar(sneg, BK[6][:, 0:64], -1.0, None, ALU.mult).inc("Vs")
            p.wait("tensor", "Vs")
            p.t.transpose(bkbf(7)[0:64, 0:128], sneg, ident_sb[:]).inc("Ts")
            p.wait("vector", "Ts")
            p.v.tensor_copy(negs[0:64, ts_ * 128:(ts_ + 1) * 128],
                            bkbf(7)[0:64, 0:128]).inc("Vs")
    def iv(tensor, off_i32, pdim, n):
        return tensor[0:pdim, 2 * off_i32:2 * (off_i32 + n)].bitcast(I32)
    if "p5" in ph:
        qint = [iv(xbuf, r * 2048, 128, 2048) for r in range(2)]
        srep8 = [fv(xbuf, 4096 + r * 2048, 128, 2048) for r in range(2)]
        deq8 = [scrA[:, 5632 + r * 2048:5632 + (r + 1) * 2048] for r in range(2)]
        x1rot = [scrA[:, 13824 + r * 512:13824 + (r + 1) * 512] for r in range(2)]
        ob32 = [fv(scrB, 0, 128, 512), fv(scrB, 512, 128, 512)]
        facc = [fv(scrB, 5120 + i * 512, 128, 512) for i in range(4)]
        for ng in range(16):
            n0 = ng * 256
            for sub in range(4):
                r = sub % 2
                c0 = sub * 8
                src = wTq1_d.rearrange("(c p) n -> p c n", p=128)[
                    :, c0:c0 + 8, n0:n0 + 256]
                dma("sync", qint[r].rearrange("p (c n) -> p c n", n=256),
                    src, "Dg5")
                src = sexp_d.rearrange("(c p) n -> p c n", p=128)[
                    :, c0:c0 + 8, n0:n0 + 256]
                dma("sync", srep8[r].rearrange("p (c n) -> p c n", n=256),
                    src, "Dg5")
                sptok = p.snap("Dg5")
                p.wait("vector", "Dg5", sptok)
                p.wait("vector", "Ts")
                p.v.tensor_tensor(deq8[r], qint[r], srep8[r], ALU.mult).inc("Vs")
                if DEBUG_P5 and ng == 15 and sub == 3:
                    p.v.tensor_copy(scrB[:, 2048:4096], qint[r])
                    p.v.tensor_copy(scrB[:, 4096:6144], srep8[r])
                    p.v.tensor_copy(scrB[:, 6144:8192], deq8[r]).inc("Vs")
                p.wait("tensor", "Vs")
                for c8 in range(8):
                    for ns in range(2):
                        for th in range(2):
                            hh = p.t.matmul(
                                BK[ns * 2 + th][:],
                                deq8[r][:, c8 * 256 + ns * 128:c8 * 256 + (ns + 1) * 128],
                                nb(sub * 8 + c8, th * 512, 512),
                                start=(c8 == 0), stop=(c8 == 7),
                                skip_group_check=True)
                hh.inc("Ts")
                p.wait("vector", "Ts")
                for ns in range(2):
                    for th in range(2):
                        i4 = ns * 2 + th
                        if sub == 0:
                            p.v.tensor_copy(facc[i4], BK[i4][:])
                        else:
                            p.v.scalar_tensor_tensor(facc[i4], BK[i4][:], 1.0,
                                                     facc[i4], ALU.mult, ALU.add)
                p.lastinc("vector", "Vs")
                p.wait("tensor", "Vs")
            for ns in range(2):
                nchunk = ng * 2 + ns
                for th in range(2):
                    p.wait("tensor", "Vs")
                    hh = p.t.matmul(BK[4 + th][:],
                                    mT_b[0:64, nchunk * 128:(nchunk + 1) * 128],
                                    negs[0:64, th * 512:(th + 1) * 512],
                                    start=True, stop=True, skip_group_check=True)
                    hh.inc("Ts")
                    r = th
                    p.wait("sync", "Vs")
                    xtok = dma("sync", x1rot[r],
                               x1f_io.rearrange("(c p) t -> p c t", p=128)[
                                   :, nchunk:nchunk + 1, th * 512:(th + 1) * 512
                               ].squeeze(1), "Dy2")
                    p.wait("vector", "Ts")
                    p.wait("vector", "Dy2", xtok)
                    p.wait("vector", "Do")
                    i4 = ns * 2 + th
                    p.v.scalar_tensor_tensor(ob32[r], BK[4 + th][:], 1.0,
                                             facc[i4], ALU.mult, ALU.add)
                    p.v.tensor_scalar(ob32[r], ob32[r],
                                      bcol[:, 64 + nchunk:65 + nchunk], None, ALU.add)
                    p.v.scalar_tensor_tensor(ob32[r], x1rot[r], 1.0, ob32[r],
                                             ALU.mult, ALU.add).inc("Vs")
                    p.wait("sync", "Vs")
                    dma("sync", out_d[nchunk * 128:(nchunk + 1) * 128,
                                      th * 512:(th + 1) * 512], ob32[r], "Do")
        ctx = fv(xbuf, 0, 128, 8192)
        vhm = xbuf[0:32, 22528:30720]
        prob_f = fv(scrB, 4096, 32, 512)
        ssum = fv(scrB, 4608, 32, 16)
        prob_b = scrB[0:32, 9248:9760]
        probT = scrB[0:32, 9760:10272]
        for gr in range(4 if DO_A else 0):
            t0 = gr * 256
            p.wait("sync", "Ts")
            for q4 in range(4):
                src = q1_d.rearrange("(c p) t -> p c t", p=128)[
                    :, q4 * 8:(q4 + 1) * 8, t0:t0 + 256]
                dma("sync", scrA[:, 0:8192].rearrange("p (c t) -> p c t", t=256)[
                    :, q4 * 8:(q4 + 1) * 8, :], src, "Dq")
                src = k1_d.rearrange("(c p) t -> p c t", p=128)[
                    :, q4 * 8:(q4 + 1) * 8, t0:t0 + 256]
                dma("sync", scrA[:, 8192:16384].rearrange("p (c t) -> p c t", t=256)[
                    :, q4 * 8:(q4 + 1) * 8, :], src, "Dq")
            p.wait("sync", "Dg2")
            dma("sync", scrB[:, 0:8192], v1t_d[:, (gr * 2) * D:(gr * 2 + 2) * D], "Dq")
            gtok = p.snap("Dq")
            p.wait("tensor", "Dq", gtok)
            p.wait("gpsimd", "Dq", gtok)
            p.wait("vector", "Da")
            for mc in range(4):
                p.wait("gpsimd", "Ts")
                for h2 in range(NH):
                    src = scrB[(mc % 2) * 64:(mc % 2) * 64 + 64,
                               (mc // 2) * D + h2 * 128:(mc // 2) * D + (h2 + 1) * 128]
                    dma("gpsimd", vhm[h2:h2 + 1, :], src, "Dg2")
                vtok_ = p.snap("Dg2")
                p.wait("tensor", "Dg2", vtok_)
                for b16 in range(4):
                    sb = BK[b16 % 2]
                    cb = BK[2 + (b16 % 2)]
                    p.wait("tensor", "Ss")
                    p.wait("tensor", "Vs")
                    for tt in range(16):
                        tl = mc * 64 + b16 * 16 + tt
                        qap = scrA[:, 0:8192].rearrange("p (c t) -> p c t", t=256)[
                            :, :, tl:tl + 1].squeeze(2)
                        kap = scrA[:, 8192:16384].rearrange("p (c t) -> p c t", t=256)[
                            :, :, tl:tl + 1].squeeze(2)
                        hh = p.t.matmul(sb[0:32, tt * 32:(tt + 1) * 32], qap, kap,
                                        start=True, stop=True, skip_group_check=True)
                    hh.inc("Ts")
                    p.wait("scalar", "Ts")
                    p.wait("scalar", "Vs")
                    p.s.activation(prob_f, sb[0:32, :], AF.Exp, scale=SCALE_ATT,
                                   bias=cexp[0:32, 1:2]).inc("Ss")
                    p.wait("vector", "Ss")
                    p.v.reduce_sum(ssum.rearrange("p (t o) -> p t o", o=1),
                                   prob_f.rearrange("p (t h) -> p t h", h=32),
                                   axis=AX.X)
                    p.v.reciprocal(ssum, ssum)
                    p.v.tensor_tensor(prob_b.rearrange("p (t h) -> p t h", h=32),
                                      prob_f.rearrange("p (t h) -> p t h", h=32),
                                      ssum.rearrange("p (t o) -> p t o",
                                                     o=1).broadcast_to([32, 16, 32]),
                                      ALU.mult)
                    p.v.transpose(probT, prob_b).inc("Vs")
                    p.wait("tensor", "Vs")
                    for tt in range(16):
                        tl = mc * 64 + b16 * 16 + tt
                        hh = p.t.matmul(cb[:, tt * 32:(tt + 1) * 32],
                                        vhm[:, (tl % 64) * 128:((tl % 64) + 1) * 128],
                                        probT[:, tt * 32:(tt + 1) * 32],
                                        start=True, stop=True, skip_group_check=True)
                    hh.inc("Ts")
                    p.wait("vector", "Ts")
                    dst = ctx.rearrange("p (h t) -> p h t", t=256)[
                        :, :, mc * 64 + b16 * 16:mc * 64 + b16 * 16 + 16]
                    p.v.tensor_copy(dst, cb[:].rearrange("p (t h) -> p h t",
                                                         h=32)).inc("Vs")
            p.wait("gpsimd", "Vs")
            p.wait("gpsimd", "Do")
            if DEBUG_P5:
                dma("gpsimd", dbg5_d[:, gr * 8192:(gr + 1) * 8192].rearrange(
                    "p n -> p n"), fv(xbuf, 0, 128, 8192), "Da")
            for h in range(NH):
                src = ctx[:, h * 256:(h + 1) * 256]
                dma("gpsimd", out_d[h * 128:(h + 1) * 128, t0:t0 + 256], src,
                    "Da", accum_op=ALU.add)

    if "p5" in ph and DEBUG_P5:
        p.wait("sync", "Vs")
        dma("sync", dbg6_d[:, 0:6144], scrB[:, 2048:8192], "Dy")
    for sem in list(p.count.keys()):
        if sem.startswith("D"):
            p.wait("sync", sem)

    p.build(nc, es)
    return nc, es, ins, outs


def kernel(**inputs):
    pre = prep_host(inputs)
    shared = {k: pre[k] for k in
              ("wk", "wv", "w_qh", "wo", "wcat", "basecat", "gate", "wbr",
               "wq1h", "wk1h", "wv1h", "bq1", "bk1", "bv1", "wTq1", "sexp", "g2in",
               "scaleT", "zpT", "bias1", "ident")}
    percore = {"x_own": pre["x_own"], "padsc": pre["padsc"], "bexp": pre["bexp"]}

    for group in PLAN:
        nc, es, ins, outs = emit_program(group)
        in_maps = []
        for c in range(NCORE):
            m = {}
            for name in ins:
                if name in shared:
                    m[name] = shared[name]
                elif name in percore:
                    m[name] = percore[name][c]
                else:
                    raise KeyError(f"missing input {name}")
            in_maps.append(m)
        res = run_bass_kernel_spmd(nc, in_maps, core_ids=list(range(NCORE)))
        es.__exit__(None, None, None)
        for name in outs:
            percore[name] = [res.results[c][name] for c in range(NCORE)]
        if "k_own" in outs:
            k_all, v_all = assemble_kv(percore["k_own"], percore["v_own"])
            percore["k_all"] = k_all
            percore["v_all"] = v_all

    out_T = np.zeros((D, T), dtype=np.float32)
    for qt in range(16):
        c = qt if qt < 8 else 15 - qt
        lo = 0 if qt < 8 else 512
        out_T[:, qt * 512:(qt + 1) * 512] = \
            np.asarray(percore["out"][c], np.float32)[:, lo:lo + 512]
    return np.ascontiguousarray(out_T.T).reshape(1, T, D)


# version 08
# speedup vs baseline: 1.6458x; 1.6458x over previous
import contextlib
from collections import defaultdict

import numpy as np

import concourse.bass as bass
import concourse.mybir as mybir
from concourse.bass_utils import run_bass_kernel_spmd

F32 = mybir.dt.float32
BF16 = mybir.dt.bfloat16
I32 = mybir.dt.int32
AF = mybir.ActivationFunctionType
AX = mybir.AxisListType
ALU = mybir.AluOpType

NCORE = 8
T = 8192
D = 4096
TPC = 1024
NT = TPC // 128
NH = 32
HD = 128
NKC = D // 128
QT = 512
LA, LB = 32, 64
NSLOT = LA + LB + 8
SCALE_ATT = 1.0 / float(np.sqrt(np.float32(128.0)))
EPS_LN = 1e-5
EPS_RMS = 1.1920929e-07
SINK_IT = 50
EXPC0 = 15.0
EXPC1 = 5.0

ENGS = ("tensor", "vector", "scalar", "gpsimd", "sync")

PLAN = [["p1", "p2", "p3a", "p3b", "p4", "p5"]]
RACE_DETECT = True
DEBUG_P3B = False
DEBUG_P5 = False
P5_PARTS = "pfa"
P5F_SERIAL = False


class _RInstr:
    __slots__ = ("meth", "args", "kwargs", "incs")

    def __init__(self, meth, args, kwargs):
        self.meth = meth
        self.args = args
        self.kwargs = kwargs
        self.incs = []


class _Handle:
    __slots__ = ("p", "ri")

    def __init__(self, p, ri):
        self.p = p
        self.ri = ri

    def inc(self, sem, n=1):
        if not sem.startswith("D"):
            return self
        self.p.count[sem] += n
        self.ri.incs.append((sem, n, self.p.count[sem]))
        return self


_TRACKED = {
    "tensor_copy", "tensor_tensor", "tensor_add", "tensor_sub", "tensor_mul",
    "tensor_max", "tensor_scalar", "tensor_single_scalar", "tensor_scalar_add",
    "tensor_scalar_sub", "tensor_scalar_mul", "tensor_scalar_max",
    "tensor_scalar_min", "tensor_relu", "scalar_tensor_tensor", "tensor_reduce",
    "reduce_sum", "reduce_max", "reciprocal", "select", "transpose", "memset",
    "memzero", "activation", "iota", "copy", "add", "mul", "sqrt", "square",
    "matmul", "ldweights",
}
_ESEM = {"tensor": "Et", "vector": "Ev", "scalar": "Es", "gpsimd": "Eg",
         "sync": "Ey"}


def _ap_span(ap):
    try:
        tname = ap.tensor.name
        esz = {mybir.dt.float32: 4, mybir.dt.int32: 4,
               mybir.dt.bfloat16: 2}.get(ap.dtype, 4)
        lo = ap.offset
        hi = ap.offset
        for step, count in ap.ap:
            if count > 1 and step > 0:
                hi += step * (count - 1)
        return (tname, lo * esz, (hi + 1) * esz)
    except Exception:
        return None


class _Eng:
    def __init__(self, p, name):
        object.__setattr__(self, "_p", p)
        object.__setattr__(self, "_name", name)

    def __getattr__(self, meth):
        p = self._p
        name = self._name

        def call(*a, **k):
            if meth in _TRACKED:
                outs, ins_ = [], []
                for idx, arg in enumerate(a):
                    sp = _ap_span(arg) if hasattr(arg, "tensor") else None
                    if sp is not None:
                        if idx == 0 and meth != "ldweights":
                            outs.append(sp)
                        else:
                            ins_.append(sp)
                for kw, arg in k.items():
                    sp = _ap_span(arg) if hasattr(arg, "tensor") else None
                    if sp is not None:
                        (outs if kw == "accum_out" else ins_).append(sp)
                if meth == "matmul" and a and k.get("start") is not True:
                    sp = _ap_span(a[0])
                    if sp is not None:
                        ins_.append(sp)
                p.auto_edges(name, ins_, outs)
                ri = _RInstr(meth, a, k)
                p.q[name].append(ri)
                p.note_access(name, ins_, outs)
                return _Handle(p, ri)
            if meth == "dma_start":
                out_sp = _ap_span(k.get("out"))
                in_sp = _ap_span(k.get("in_"))
                p.auto_edges(name, [in_sp] if in_sp else [],
                             [out_sp] if out_sp else [])
                ri = _RInstr(meth, a, k)
                p.q[name].append(ri)
                p.pending_dma[name] = (ri, in_sp, out_sp)
                return _Handle(p, ri)
            if meth == "drain":
                p.pending[name].clear()
            ri = _RInstr(meth, a, k)
            p.q[name].append(ri)
            return _Handle(p, ri)

        return call


class Prog:
    def __init__(self):
        self.q = {e: [] for e in ENGS}
        self.count = defaultdict(int)
        self.hw = defaultdict(int)
        self.pending = {e: [] for e in ENGS}
        self.pending_dma = {e: None for e in ENGS}
        self.lastw = defaultdict(list)
        self.lastr = defaultdict(list)
        self.t = _Eng(self, "tensor")
        self.v = _Eng(self, "vector")
        self.s = _Eng(self, "scalar")
        self.g = _Eng(self, "gpsimd")
        self.y = _Eng(self, "sync")

    def _hits(self, store, sp):
        tn, lo, hi = sp
        return [e for e in store[tn] if e[0] < hi and lo < e[1]]

    def auto_edges(self, eng, ins_, outs):
        need_drain = False
        for sp in ins_:
            for (lo, hi, sem, cnt, e2) in self._hits(self.lastw, sp):
                if e2 == eng and sem in _ESEM.values():
                    if eng != "tensor":
                        need_drain = True
                else:
                    self.wait(eng, sem, cnt)
        for sp in outs:
            for (lo, hi, sem, cnt, e2) in self._hits(self.lastw, sp) +                     self._hits(self.lastr, sp):
                if e2 == eng and sem in _ESEM.values():
                    pass
                else:
                    self.wait(eng, sem, cnt)
        if need_drain:
            self.q[eng].append(_RInstr("drain", (), {}))

    def note_access(self, eng, ins_, outs, sem=None, amt=1):
        if sem is None:
            sem = _ESEM[eng]
        if outs:
            self.count[sem] += amt
            ri = self.q[eng][-1]
            assert isinstance(ri, _RInstr)
            ri.incs.append((sem, amt, self.count[sem]))
        cnt = self.count[sem]
        for sp in outs:
            tn, lo, hi = sp
            keep = [e for e in self.lastw[tn] if not (e[0] < hi and lo < e[1]
                                                      )]
            keep.append((lo, hi, sem, cnt, eng))
            self.lastw[tn] = keep[-4096:]
            self.lastr[tn] = [e for e in self.lastr[tn]
                              if not (e[0] < hi and lo < e[1])]
        for sp in ins_:
            tn, lo, hi = sp
            v = cnt if outs else self.count[sem]
            lst = self.lastr[tn]
            for i in range(len(lst) - 1, -1, -1):
                e = lst[i]
                if e[0] == lo and e[1] == hi and e[2] == sem and e[4] == eng:
                    del lst[i]
            lst.append((lo, hi, sem, v, eng))
            if len(lst) > 4096:
                del lst[0:1024]

    _MANUAL_DEAD = {"Ts", "Vs", "Ss", "Gs", "Ys"}

    def wait(self, eng, sem, val=None):
        if sem in self._MANUAL_DEAD:
            return
        if val is None:
            val = self.count[sem]
        if val <= 0 or val <= self.hw[(eng, sem)]:
            return
        self.hw[(eng, sem)] = val
        self.q[eng].append(("wait", sem, val))

    def snap(self, sem):
        return self.count[sem]

    def lastinc(self, eng, sem, n=1):
        if not sem.startswith("D"):
            return
        ri = self.q[eng][-1]
        assert isinstance(ri, _RInstr)
        self.count[sem] += n
        ri.incs.append((sem, n, self.count[sem]))

    def build(self, nc, es):
        import bisect
        esems = set(_ESEM.values())
        waited = defaultdict(set)
        for e in ENGS:
            for it in self.q[e]:
                if isinstance(it, tuple) and it[1] in esems:
                    waited[it[1]].add(it[2])
        wl = {sname: sorted(v) for sname, v in waited.items()}

        def rank(sname, v):
            return bisect.bisect_right(wl.get(sname, []), v)

        newcount = defaultdict(int)
        for e in ENGS:
            nq = []
            for it in self.q[e]:
                if isinstance(it, tuple):
                    _, sname, val = it
                    if sname in esems:
                        nv = rank(sname, val)
                        if nv > 0:
                            nq.append(("wait", sname, nv))
                    else:
                        nq.append(it)
                else:
                    kept = []
                    for (sname, n, ca) in it.incs:
                        if sname in esems:
                            n2 = rank(sname, ca) - rank(sname, ca - n)
                            if n2 > 0:
                                kept.append((sname, n2))
                                newcount[sname] = max(newcount[sname],
                                                      rank(sname, ca))
                        else:
                            kept.append((sname, n))
                            newcount[sname] += n
                    it.incs = kept
                    nq.append(it)
            self.q[e] = nq
        for name, c in self.count.items():
            if name not in esems:
                assert c < 60000, f"sem {name} count {c} too high"
        for name, c in newcount.items():
            assert c < 60000, f"sem {name} count {c} too high"
        used = set()
        for e in ENGS:
            for it in self.q[e]:
                if isinstance(it, tuple):
                    used.add(it[1])
                else:
                    for t in it.incs:
                        used.add(t[0])
        sems = {name: es.enter_context(nc.semaphore(name))
                for name in self.count.keys() if name in used}
        blk = es.enter_context(nc.Block())

        def make_replay(entries):
            def replay(eng_obj):
                for it in entries:
                    if isinstance(it, tuple):
                        _, sem, val = it
                        eng_obj.wait_ge(sems[sem], val)
                    else:
                        inst = getattr(eng_obj, it.meth)(*it.args, **it.kwargs)
                        for tup in it.incs:
                            inst.then_inc(sems[tup[0]], tup[1])
            return replay

        for e in ENGS:
            if self.q[e]:
                getattr(blk, e)(make_replay(self.q[e]))


def prep_host(inputs):
    f32 = np.float32
    x = np.ascontiguousarray(np.asarray(inputs["L0_attn_x"], f32).reshape(T, D))
    pre = {}
    rows = []
    for c in range(NCORE):
        a, bq = c, 15 - c
        rows.append(np.ascontiguousarray(np.concatenate(
            [x[a * 512:(a + 1) * 512], x[bq * 512:(bq + 1) * 512]], axis=0)))
    pre["x_own"] = rows

    pre["wk"] = np.ascontiguousarray(np.asarray(inputs["L0_attn_wk"], f32))
    pre["wv"] = np.ascontiguousarray(np.asarray(inputs["L0_attn_wv"], f32))
    wq = np.asarray(inputs["L0_attn_wq"], f32)
    pre["w_qh"] = np.ascontiguousarray(wq.reshape(D, NH, HD).transpose(1, 0, 2))
    pre["wo"] = np.ascontiguousarray(np.asarray(inputs["L0_attn_wo"], f32))
    pre["wcat"] = np.ascontiguousarray(np.concatenate(
        [np.asarray(inputs["L0_ffn_W_res"], f32).T,
         np.asarray(inputs["L0_ffn_W_pre"], f32).T,
         np.asarray(inputs["L0_ffn_W_post"], f32).T], axis=1))
    pre["basecat"] = np.ascontiguousarray(np.concatenate(
        [np.asarray(inputs["L0_ffn_H_res_base"], f32).reshape(16),
         np.asarray(inputs["L0_ffn_H_pre_base"], f32).reshape(4),
         np.asarray(inputs["L0_ffn_H_post_base"], f32).reshape(4)]))[None, :]
    pre["gate"] = np.asarray(inputs["L0_ffn_gate"], f32).reshape(1, 1)
    pre["wbr"] = np.ascontiguousarray(np.asarray(inputs["L0_ffn_W_branch"], f32))

    pre["wqT1"] = np.ascontiguousarray(np.asarray(inputs["L1_attn_wq"], f32).T)
    pre["wkT1"] = np.ascontiguousarray(np.asarray(inputs["L1_attn_wk"], f32).T)
    pre["wvT1"] = np.ascontiguousarray(np.asarray(inputs["L1_attn_wv"], f32).T)
    pre["wq1h"] = np.ascontiguousarray(pre["wqT1"].reshape(D, 32, 128).transpose(1, 0, 2))
    pre["wk1h"] = np.ascontiguousarray(pre["wkT1"].reshape(D, 32, 128).transpose(1, 0, 2))
    pre["wv1h"] = np.ascontiguousarray(pre["wvT1"].reshape(D, 32, 128).transpose(1, 0, 2))
    pre["bq1"] = np.ascontiguousarray(np.asarray(inputs["L1_attn_bq"], f32).reshape(D, 1))
    pre["bk1"] = np.ascontiguousarray(np.asarray(inputs["L1_attn_bk"], f32).reshape(D, 1))
    pre["bv1"] = np.ascontiguousarray(np.asarray(inputs["L1_attn_bv"], f32).reshape(1, D))

    pre["wTq1"] = np.ascontiguousarray(np.asarray(inputs["L1_ffn_weight_q"], np.int32).T)
    scale = np.asarray(inputs["L1_ffn_scale"], f32)
    zp = np.asarray(inputs["L1_ffn_zp"], f32)
    pre["spair"] = np.ascontiguousarray(scale.T.reshape(32, 2, D).transpose(1, 0, 2))
    pre["sexp"] = np.ascontiguousarray(np.repeat(scale.T, 64, axis=0))
    pre["scaleT"] = np.ascontiguousarray(scale.T)
    pre["zpT"] = np.ascontiguousarray(zp.T)
    pre["bias1"] = np.ascontiguousarray(np.asarray(inputs["L1_ffn_bias"], f32).reshape(D, 1))
    pre["ident"] = np.eye(128, dtype=f32)
    g2 = np.zeros((2, 128), dtype=f32)
    g2[0, 0:64] = 1.0
    g2[1, 64:128] = 1.0
    pre["g2in"] = g2
    pre["padsc"] = [np.zeros((1, 2), dtype=f32) for c in range(NCORE)]
    bexps = []
    for c in range(NCORE):
        nA, nB = 4 * c, 60 - 4 * c
        b = np.full(NSLOT, -115.0, dtype=f32)
        b[0:nA] = -EXPC0
        b[LA:LA + nB] = -EXPC0
        b[LA + LB:] = -EXPC0
        bexps.append(np.ascontiguousarray(np.broadcast_to(b[None, :], (128, NSLOT)).copy()))
    pre["bexp"] = bexps
    return pre


def assemble_kv(k_own_list, v_own_list):
    import ml_dtypes
    bfd = ml_dtypes.bfloat16
    k_glob = np.zeros((HD, T), dtype=bfd)
    v_glob = np.zeros((T, HD), dtype=bfd)
    for c in range(NCORE):
        ko = np.asarray(k_own_list[c])
        vo = np.asarray(v_own_list[c])
        vtok = vo.reshape(128, NT, HD).transpose(1, 0, 2).reshape(TPC, HD)
        for li, qt in enumerate((c, 15 - c)):
            k_glob[:, qt * 512:(qt + 1) * 512] = ko[:, li * 512:(li + 1) * 512]
            v_glob[qt * 512:(qt + 1) * 512] = vtok[li * 512:(li + 1) * 512]
    k_all, v_all = [], []
    for c in range(NCORE):
        nA, nB = 4 * c, 60 - 4 * c
        k = np.zeros((HD, NSLOT * 128), dtype=bfd)
        v = np.zeros((128, NSLOT * 128), dtype=bfd)
        k[:, 0:nA * 128] = k_glob[:, 0:nA * 128]
        k[:, LA * 128:(LA + nB) * 128] = k_glob[:, 0:nB * 128]
        qa, qb = c, 15 - c
        k[:, (LA + LB) * 128:(LA + LB + 4) * 128] = k_glob[:, qa * 512:(qa + 1) * 512]
        k[:, (LA + LB + 4) * 128:] = k_glob[:, qb * 512:(qb + 1) * 512]

        def vslot(dst_slot, gchunk):
            v[:, dst_slot * 128:(dst_slot + 1) * 128] = \
                v_glob[gchunk * 128:(gchunk + 1) * 128, :]
        for j in range(nA):
            vslot(j, j)
        for j in range(nB):
            vslot(LA + j, j)
        for j in range(4):
            vslot(LA + LB + j, qa * 4 + j)
            vslot(LA + LB + 4 + j, qb * 4 + j)
        k_all.append(np.ascontiguousarray(k))
        v_all.append(np.ascontiguousarray(v))
    return k_all, v_all


def emit_program(group):
    ph = set(group)
    nc = bass.Bass("TRN2", target_bir_lowering=False,
                   detect_race_conditions=RACE_DETECT)
    es = contextlib.ExitStack()
    es.__enter__()
    p = Prog()
    ins = {}
    outs = {}

    def din(name, shape, dtype=F32):
        t = nc.dram_tensor(name, list(shape), dtype, kind="ExternalInput").ap()
        ins[name] = t
        return t

    def dout(name, shape, dtype=F32):
        t = nc.dram_tensor(name, list(shape), dtype, kind="ExternalOutput").ap()
        outs[name] = t
        return t

    def dint(name, shape, dtype=F32):
        return nc.dram_tensor(name, list(shape), dtype, kind="Internal").ap()

    def sbuf(name, shape, dtype):
        return es.enter_context(nc.sbuf_tensor(name, list(shape), dtype))

    def psum(name, shape, dtype=F32):
        return es.enter_context(nc.psum_tensor(name, list(shape), dtype))

    def handoff(name, shape, dtype, produced, consumed):
        prod = any(q in ph for q in produced)
        cons = any(q in ph for q in consumed)
        if prod and cons:
            return dint(name + "_i", shape, dtype), "int"
        if prod:
            return dout(name, shape, dtype), "out"
        if cons:
            return din(name, shape, dtype), "in"
        return None, "none"

    if ph & {"p1", "p2"}:
        x_own = din("x_own", (TPC, D))
    if "p1" in ph:
        wk_d = din("wk", (D, HD))
        wv_d = din("wv", (D, HD))
    if "p2" in ph:
        wcat_d = din("wcat", (D, 24))
        base_d = din("basecat", (1, 24))
        gate_d = din("gate", (1, 1))
        wbr_d = din("wbr", (1024, 1024))
    if "p3a" in ph:
        wqh_d = din("w_qh", (NH, D, HD))
    FUSED_KV = ("p1" in ph) and ("p3b" in ph)
    if "p3b" in ph:
        wo_d = din("wo", (D, D))
        padsc_d = din("padsc", (1, 2))
        bexp_d = din("bexp", (128, NSLOT))
        if DEBUG_P3B:
            dbg_d = dout("dbg", (128, 2048))
        if FUSED_KV:
            k_ag_d = dint("k_ag", (NCORE * HD, TPC), BF16)
            v_ag_d = dint("v_ag", (NCORE * 128, NT * HD), BF16)
        else:
            k_all_d = din("k_all", (HD, NSLOT * 128), BF16)
            v_all_d = din("v_all", (128, NSLOT * 128), BF16)
    ident_d = din("ident", (128, 128))
    if "p5" in ph:
        wq1h_d = din("wq1h", (32, D, 128))
        wk1h_d = din("wk1h", (32, D, 128))
        wv1h_d = din("wv1h", (32, D, 128))
        bq1_d = din("bq1", (D, 1))
        bk1_d = din("bk1", (D, 1))
        bv1_d = din("bv1", (1, D))
        wTq1_d = din("wTq1", (D, D), I32)
        sexp_d = din("sexp", (D, D))
        scaleT_d = din("scaleT", (64, D))
        zpT_d = din("zpT", (64, D))
        bias1_d = din("bias1", (D, 1))
        g2_d = din("g2in", (2, 128))
        out_d = dout("out", (D, TPC))
        mk5 = dout if DEBUG_P5 else dint
        if DEBUG_P5:
            dbg5_d = dout("dbg5", (128, 4 * 8192))
            dbg6_d = dout("dbg6", (128, 6144), BF16)
        q1_d = mk5("q1_scr", (D, TPC), BF16)
        k1_d = mk5("k1_scr", (D, TPC), BF16)
        v1t_d = mk5("v1t_scr", (128, NT * D), BF16)

    xn_io, xn_mode = handoff("xn_h", (D, TPC), BF16, ["p1"], ["p2", "p3a"])
    var_io, var_mode = handoff("var_h", (128, NT), F32, ["p1"], ["p2"])
    if "p1" in ph:
        if "p3b" in ph:
            k_own_io = dint("k_own_i", (HD, TPC), BF16)
            v_own_io = dint("v_own_i", (128, NT * HD), BF16)
        else:
            k_own_io = dout("k_own", (HD, TPC), BF16)
            v_own_io = dout("v_own", (128, NT * HD), BF16)
    xf_io, xf_mode = handoff("xf_h", (128, NT * D), BF16, ["p2"], ["p3b"])
    q_io, q_mode = handoff("q0_h", (D, TPC), BF16, ["p3a"], ["p3b"])
    x1_io, x1_mode = handoff("x1_h", (128, NT * D), BF16, ["p3b"], ["p4"])
    xn1_io, xn1_mode = handoff("xn1_h", (D, TPC), BF16, ["p4"], ["p5"])
    x1f_io, x1f_mode = handoff("x1f_h", (D, TPC), BF16, ["p4"], ["p5"])

    xbuf = sbuf("xbuf", (128, NT * D), BF16)
    nbuf = sbuf("nbuf", (128, NT * D), BF16)
    scrA = sbuf("scrA", (128, 16384), BF16)
    scrB = sbuf("scrB", (128, 16384), BF16)
    stat = sbuf("stat", (128, 32), F32)
    ident_sb = sbuf("ident_sb", (128, 128), BF16)
    ones_rowf = sbuf("ones_rowf", (1, 128), F32)
    ones_colb = sbuf("ones_colb", (128, 1), BF16)
    cexp = sbuf("cexp", (128, 4), F32)
    misc = sbuf("misc", (128, 64), F32)
    rowf = sbuf("rowf", (1, 512), F32)
    if ph & {"p1", "p2", "p4"}:
        vsave = sbuf("vsave", (128, NT), F32)
    if "p2" in ph:
        lgbuf = sbuf("lgbuf", (128, 192), F32)
    if "p3b" in ph:
        pass
    if "p5" in ph:
        ones_rowb = sbuf("ones_rowb", (1, 512), BF16)
        bcol = sbuf("bcol", (128, 96), F32)
        ew = sbuf("ew", (128, 192), BF16)
        g2 = sbuf("g2", (2, 128), BF16)
        brow = sbuf("brow", (1, 256), BF16)

    BK = [psum(f"pbk{i}", (128, 512), F32) for i in range(8)]

    def bkbf(i):
        return BK[i][:].bitcast(BF16)

    def nb(c, t0, n):
        return nbuf[:, c * TPC + t0:c * TPC + t0 + n]

    def xbt(i, f0, n):
        return xbuf[:, i * D + f0:i * D + f0 + n]

    def fv(tensor, off_f32, pdim, n):
        return tensor[0:pdim, 2 * off_f32:2 * (off_f32 + n)].bitcast(F32)

    p.v.memset(ones_rowf[:], 1.0)
    if "p5" in ph:
        p.v.memset(ones_rowb[:], 1.0)
    p.v.memset(ones_colb[:], 1.0)
    p.v.memset(cexp[:, 0:1], -EXPC0)
    p.v.memset(cexp[:, 1:2], -EXPC1)
    p.v.memset(cexp[:, 2:3], EPS_LN)
    p.v.memset(cexp[:, 3:4], EPS_RMS).inc("Vs")

    _EM = {"gpsimd": p.g, "sync": p.y, "scalar": p.s, "vector": p.v, "tensor": p.t}

    def dma(eng, out, in_, sem, **kw):
        h = _EM[eng].dma_start(out=out, in_=in_, **kw)
        h.inc(sem, 16)
        ri, in_sp, out_sp = p.pending_dma[eng]
        cnt = p.count[sem]
        for sp in ([out_sp] if out_sp else []):
            tn, lo, hi = sp
            keep = [e for e in p.lastw[tn] if not (e[0] < hi and lo < e[1])]
            keep.append((lo, hi, sem, cnt, eng))
            p.lastw[tn] = keep[-4096:]
            p.lastr[tn] = [e for e in p.lastr[tn]
                           if not (e[0] < hi and lo < e[1])]
        for sp in ([in_sp] if in_sp else []):
            tn, lo, hi = sp
            lst = p.lastr[tn]
            for i in range(len(lst) - 1, -1, -1):
                e = lst[i]
                if e[0] == lo and e[1] == hi and e[2] == sem and e[4] == eng:
                    del lst[i]
            lst.append((lo, hi, sem, cnt, eng))
        return cnt

    dma("gpsimd", ident_sb[:], ident_d, "Dg")

    def stats_ln(src_ap, sq_scr, save_var_col=None):
        mu = stat[:, 0:1]
        s1 = stat[:, 1:2]
        s2 = stat[:, 2:3]
        ex2 = stat[:, 3:4]
        var = stat[:, 4:5]
        rstd = stat[:, 5:6]
        p.v.reduce_sum(s1, src_ap, axis=AX.X).inc("Vs")
        p.s.activation(sq_scr, src_ap, AF.Square, accum_out=s2).inc("Ss")
        p.wait("vector", "Ss")
        p.v.tensor_scalar_mul(mu, s1, 1.0 / D)
        p.v.tensor_scalar_mul(ex2, s2, 1.0 / D)
        p.v.tensor_tensor(var, mu, mu, ALU.mult)
        p.v.tensor_tensor(var, ex2, var, ALU.subtract)
        if save_var_col is not None:
            p.v.tensor_copy(save_var_col, var)
        p.lastinc("vector", "Vs")
        p.wait("scalar", "Vs")
        p.s.activation(rstd, var, AF.Ln, bias=cexp[:, 2:3])
        p.s.activation(rstd, rstd, AF.Exp, scale=-0.5).inc("Ss")
        p.wait("vector", "Ss")
        return mu, rstd

    def transpose_32(src_tile_fn, dst_fn):
        for grpi in range(8):
            bk = 6 + (grpi % 2)
            c0 = grpi * 4
            p.wait("tensor", "Vs")
            for j in range(4):
                h = p.t.transpose(bkbf(bk)[:, j * 128:(j + 1) * 128],
                                  src_tile_fn(c0 + j), ident_sb[:])
            h.inc("Ts")
            p.wait("vector", "Ts")
            src = bkbf(bk)[:, 0:512].rearrange("p (a b) -> p a b", b=128)
            p.v.tensor_copy(dst_fn(c0), src).inc("Vs")

    def export_fm(dst_dram, sem):
        for q4 in range(4):
            src = nbuf[:].rearrange("p (c t) -> p c t", t=TPC)[:, q4 * 8:(q4 + 1) * 8, :]
            dst = dst_dram.rearrange("(c p) t -> p c t", p=128)[:, q4 * 8:(q4 + 1) * 8, :]
            dma("sync", dst, src, sem)

    def import_fm(src_dram, sem):
        for q4 in range(4):
            dst = nbuf[:].rearrange("p (c t) -> p c t", t=TPC)[:, q4 * 8:(q4 + 1) * 8, :]
            src = src_dram.rearrange("(c p) t -> p c t", p=128)[:, q4 * 8:(q4 + 1) * 8, :]
            dma("sync", dst, src, sem)

    if "p1" in ph:
        for q4 in range(4):
            src = wk_d.rearrange("(c p) d -> p c d", p=128)[:, q4 * 8:(q4 + 1) * 8, :]
            dst = scrB[:, 0:4096].rearrange("p (c d) -> p c d", d=128)[:, q4 * 8:(q4 + 1) * 8, :]
            dma("gpsimd", dst, src, "Dg")
            src = wv_d.rearrange("(c p) d -> p c d", p=128)[:, q4 * 8:(q4 + 1) * 8, :]
            dst = scrB[:, 4096:8192].rearrange("p (c d) -> p c d", d=128)[:, q4 * 8:(q4 + 1) * 8, :]
            dma("gpsimd", dst, src, "Dg")
        wtok = p.snap("Dg")
        k_own_sb = scrA[:, 0:TPC]
        v_own_sb = scrA[:, TPC:2 * TPC]
        sqs = scrA[:, 2048:6144]
        vstage = scrA[:, 6144:6272]
        xrot = [scrB[:, 8192 + i * D:8192 + (i + 1) * D] for i in range(2)]
        for i in range(NT):
            xt = xrot[i % 2]
            p.wait("gpsimd", "Ts")
            p.wait("gpsimd", "Vs")
            p.wait("gpsimd", "Ss")
            tok = dma("gpsimd", xt, x_own[i * 128:(i + 1) * 128, :], "Dg")
            p.wait("vector", "Dg", tok)
            p.wait("scalar", "Dg", tok)
            mu, rstd = stats_ln(xt, sqs, vsave[:, i:i + 1])
            p.v.tensor_scalar(xt, xt, mu, rstd, ALU.subtract, ALU.mult).inc("Vs")
            p.wait("tensor", "Vs")
            p.wait("tensor", "Dg", wtok)
            transpose_32(
                lambda c: xt[:, c * 128:(c + 1) * 128],
                lambda c0: nbuf[:].rearrange("p (c t) -> p c t", t=TPC)[
                    :, c0:c0 + 4, i * 128:(i + 1) * 128])
            p.wait("tensor", "Vs")
            for c in range(NKC):
                hh = p.t.matmul(BK[0][:, 0:128], scrB[:, c * 128:(c + 1) * 128],
                                nb(c, i * 128, 128), start=(c == 0), stop=(c == NKC - 1))
            hh.inc("Ts")
            for c in range(NKC):
                hh = p.t.matmul(BK[1][:, 0:128], scrB[:, 4096 + c * 128:4096 + (c + 1) * 128],
                                nb(c, i * 128, 128), start=(c == 0), stop=(c == NKC - 1))
            hh.inc("Ts")
            p.wait("vector", "Ts")
            p.v.tensor_copy(k_own_sb[:, i * 128:(i + 1) * 128], BK[0][:, 0:128])
            p.v.tensor_copy(vstage, BK[1][:, 0:128])
            p.lastinc("vector", "Vs")
            p.wait("tensor", "Vs")
            p.t.transpose(bkbf(2)[:, 0:128], vstage, ident_sb[:]).inc("Ts")
            p.wait("vector", "Ts")
            p.v.tensor_copy(v_own_sb[:, i * 128:(i + 1) * 128], bkbf(2)[:, 0:128]).inc("Vs")
        p.wait("sync", "Vs")
        dma("sync", k_own_io, k_own_sb, "Dy")
        dma("sync", v_own_io, v_own_sb, "Dy")
        if xn_mode == "out":
            export_fm(xn_io, "Dy")
            dma("sync", var_io, vsave[:], "Dy")

    if "p2" in ph:
        if xn_mode == "in":
            import_fm(xn_io, "Dy")
            dma("sync", vsave[:], var_io, "Dy")
        for i in range(NT):
            dma("gpsimd", xbuf[:, i * D:(i + 1) * D], x_own[i * 128:(i + 1) * 128, :], "Dg")
        for c in range(NKC):
            dma("gpsimd", scrB[:, c * 24:(c + 1) * 24], wcat_d[c * 128:(c + 1) * 128, :], "Dg")
        dma("sync", rowf[0:1, 0:24], base_d, "Dy")
        dma("sync", misc[0:1, 30:31], gate_d, "Dy")
        p.wait("tensor", "Dy")
        p.wait("tensor", "Vs")
        p.t.matmul(BK[0][:, 0:24], ones_rowf[0:1, 0:128], rowf[0:1, 0:24],
                   start=True, stop=True).inc("Ts")
        p.wait("scalar", "Dy")
        p.s.activation(misc[0:1, 31:32], misc[0:1, 30:31], AF.Sigmoid).inc("Ss")
        p.wait("tensor", "Ss")
        p.t.matmul(BK[1][:, 0:1], ones_rowf[0:1, 0:128], misc[0:1, 31:32],
                   start=True, stop=True).inc("Ts")
        p.wait("vector", "Ts")
        p.v.tensor_copy(misc[:, 0:24], BK[0][:, 0:24])
        p.v.tensor_scalar_mul(misc[:, 0:16], misc[:, 0:16], 10.0)
        p.v.tensor_copy(misc[:, 32:33], BK[1][:, 0:1])
        p.v.tensor_scalar(misc[:, 33:34], misc[:, 32:33], -1.0, 1.0,
                          ALU.mult, ALU.add).inc("Vs")
        p.wait("vector", "Dy")
        for i in range(NT):
            t0 = stat[:, 6:7]
            p.v.tensor_scalar_add(t0, vsave[:, i:i + 1], EPS_LN)
            p.v.reciprocal(t0, t0)
            p.v.tensor_tensor(t0, vsave[:, i:i + 1], t0, ALU.mult).inc("Vs")
            p.wait("scalar", "Vs")
            p.s.activation(t0, t0, AF.Ln, bias=cexp[:, 3:4])
            p.s.activation(stat[:, 8 + i:9 + i], t0, AF.Exp, scale=-0.5).inc("Ss")
            p.wait("vector", "Ss")
        for i in range(NT):
            p.v.tensor_scalar_mul(stat[:, 16 + i:17 + i], stat[:, 8 + i:9 + i], 10.0)
        p.lastinc("vector", "Vs")
        lg = lgbuf[:]
        p.wait("tensor", "Dg")
        p.wait("tensor", "Dy")
        for i in range(NT):
            p.wait("tensor", "Vs")
            for c in range(NKC):
                hh = p.t.matmul(BK[2][:, 0:24], nb(c, i * 128, 128),
                                scrB[:, c * 24:(c + 1) * 24],
                                start=(c == 0), stop=(c == NKC - 1))
            hh.inc("Ts")
            p.wait("vector", "Ts")
            p.v.scalar_tensor_tensor(lg[:, i * 24:i * 24 + 16], BK[2][:, 0:16],
                                     stat[:, 16 + i:17 + i], misc[:, 0:16],
                                     ALU.mult, ALU.add)
            p.v.scalar_tensor_tensor(lg[:, i * 24 + 16:i * 24 + 24], BK[2][:, 16:24],
                                     stat[:, 8 + i:9 + i], misc[:, 16:24],
                                     ALU.mult, ALU.add)
            p.lastinc("vector", "Vs")
        skv = fv(scrB, 512, 128, 416)
        ub = skv[:, 0:32]
        vb = skv[:, 32:64]
        Bb = skv[:, 64:192]
        mmx = skv[:, 192:224]
        sS = skv[:, 224:256]
        lnS = skv[:, 256:288]
        Ee = skv[:, 288:416]
        p.v.memset(ub, 0.0)
        p.v.memset(vb, 0.0).inc("Vs")
        Z4 = lg.rearrange("p (s n) -> p s n", n=24)[:, :, 0:16].rearrange(
            "p s (i j) -> p s i j", j=4)
        B4 = Bb.rearrange("p (s i j) -> p s i j", i=4, j=4)
        B4ji = Bb.rearrange("p (s i j) -> p s j i", i=4, j=4)
        E4 = Ee.rearrange("p (s i j) -> p s i j", i=4, j=4)
        E4ji = Ee.rearrange("p (s i j) -> p s j i", i=4, j=4)
        m3 = mmx.rearrange("p (s i) -> p s i", i=4)
        s3 = sS.rearrange("p (s i) -> p s i", i=4)

        def bc_last(ap32):
            return ap32.rearrange("p (s i) -> p s i", i=4).unsqueeze(3).broadcast_to(
                [128, 8, 4, 4])

        def bc_mid(ap32):
            return ap32.rearrange("p (s j) -> p s j", j=4).unsqueeze(2).broadcast_to(
                [128, 8, 4, 4])

        for it in range(SINK_IT):
            p.v.tensor_tensor(B4, Z4, bc_mid(vb), ALU.add)
            p.v.reduce_max(m3, B4, axis=AX.X)
            p.v.tensor_tensor(B4, B4, bc_last(mmx), ALU.subtract).inc("Vs")
            p.wait("scalar", "Vs")
            p.s.activation(Ee, Bb, AF.Exp).inc("Ss")
            p.wait("vector", "Ss")
            p.v.reduce_sum(s3, E4, axis=AX.X).inc("Vs")
            p.wait("scalar", "Vs")
            p.s.activation(lnS, sS, AF.Ln).inc("Ss")
            p.wait("vector", "Ss")
            p.v.scalar_tensor_tensor(ub, mmx, -1.0, lnS, ALU.mult, ALU.subtract)
            p.v.tensor_tensor(B4, Z4, bc_last(ub), ALU.add)
            p.v.reduce_max(m3, B4ji, axis=AX.X)
            p.v.tensor_tensor(B4, B4, bc_mid(mmx), ALU.subtract).inc("Vs")
            p.wait("scalar", "Vs")
            p.s.activation(Ee, Bb, AF.Exp).inc("Ss")
            p.wait("vector", "Ss")
            p.v.reduce_sum(s3, E4ji, axis=AX.X).inc("Vs")
            p.wait("scalar", "Vs")
            p.s.activation(lnS, sS, AF.Ln).inc("Ss")
            p.wait("vector", "Ss")
            p.v.scalar_tensor_tensor(vb, mmx, -1.0, lnS, ALU.mult, ALU.subtract)
        p.v.tensor_tensor(B4, Z4, bc_last(ub), ALU.add)
        p.v.tensor_tensor(B4, B4, bc_mid(vb), ALU.add).inc("Vs")
        p.wait("scalar", "Vs")
        Zp = lg.rearrange("p (s n) -> p s n", n=24)[:, :, 0:16]
        p.s.activation(Zp, Bb.rearrange("p (s n) -> p s n", n=16), AF.Exp).inc("Ss")
        p.wait("vector", "Ss")
        p.v.tensor_scalar(Zp, Zp, misc[:, 32:33], None, ALU.mult)
        for j in range(4):
            dg = lg.rearrange("p (s n) -> p s n", n=24)[:, :, 5 * j:5 * j + 1]
            p.v.tensor_scalar(dg, dg, misc[:, 33:34], None, ALU.add)
        for off in (16, 20):
            P3 = lg.rearrange("p (s n) -> p s n", n=24)[:, :, off:off + 4]
            mp = mmx[:, 0:8]
            mpb = mmx[:, 0:8].rearrange("p (s o) -> p s o", o=1).broadcast_to([128, 8, 4])
            p.v.reduce_max(mp, P3, axis=AX.X)
            p.v.tensor_tensor(P3, P3, mpb, ALU.subtract).inc("Vs")
            p.wait("scalar", "Vs")
            p.s.activation(P3, P3, AF.Exp).inc("Ss")
            p.wait("vector", "Ss")
            p.v.reduce_sum(mp, P3, axis=AX.X)
            p.v.reciprocal(mp, mp)
            p.v.tensor_tensor(P3, P3, mpb, ALU.mult).inc("Vs")
        xt_tok = scrB[:, 8192:12288]
        xp32 = fv(scrB, 6144, 128, 1024)
        xpb = scrB[:, 14336:15360]
        for i in range(NT):
            transpose_32(
                lambda c: nb(c, i * 128, 128),
                lambda c0: xt_tok[:, c0 * 128:(c0 + 4) * 128].rearrange(
                    "p (a b) -> p a b", b=128))
            p.v.tensor_scalar(xp32, xt_tok[:, 0:1024], lg[:, i * 24 + 16:i * 24 + 17],
                              None, ALU.mult)
            for n in range(1, 4):
                p.v.scalar_tensor_tensor(xp32, xt_tok[:, n * 1024:(n + 1) * 1024],
                                         lg[:, i * 24 + 16 + n:i * 24 + 17 + n], xp32,
                                         ALU.mult, ALU.add)
            p.v.tensor_copy(xpb, xp32).inc("Vs")
            p.wait("tensor", "Vs")
            for grpi in range(2):
                bk = 6 + grpi
                c0 = grpi * 4
                for j in range(4):
                    h = p.t.transpose(bkbf(bk)[:, j * 128:(j + 1) * 128],
                                      xpb[:, (c0 + j) * 128:(c0 + j + 1) * 128],
                                      ident_sb[:])
                h.inc("Ts")
                p.wait("vector", "Ts")
                dst = scrA[:, 0:8192].rearrange("p (c t) -> p c t", t=1024)[
                    :, c0:c0 + 4, i * 128:(i + 1) * 128]
                p.v.tensor_copy(dst, bkbf(bk)[:, 0:512].rearrange(
                    "p (a b) -> p a b", b=128)).inc("Vs")
        p.wait("gpsimd", "Ts")
        dma("gpsimd", scrB[:, 0:8192].rearrange("p (c f) -> p c f", f=1024),
            wbr_d.rearrange("(c p) f -> p c f", p=128), "Dg")
        wtok = p.snap("Dg")
        p.wait("tensor", "Dg", wtok)
        for fc in range(8):
            for th in range(2):
                p.wait("tensor", "Vs")
                for cc in range(8):
                    hh = p.t.matmul(
                        BK[2][:],
                        scrB[:, cc * 1024 + fc * 128:cc * 1024 + (fc + 1) * 128],
                        scrA[:, cc * 1024 + th * 512:cc * 1024 + (th + 1) * 512],
                        start=(cc == 0), stop=(cc == 7))
                hh.inc("Ts")
                p.wait("vector", "Ts")
                p.v.tensor_copy(scrA[:, 8192 + fc * 1024 + th * 512:
                                     8192 + fc * 1024 + (th + 1) * 512],
                                BK[2][:]).inc("Vs")
        ytk = scrB[:, 15360:16384]
        acc = fv(scrA, 0, 128, 4096)
        for i in range(NT):
            transpose_32(
                lambda c: nb(c, i * 128, 128),
                lambda c0: xt_tok[:, c0 * 128:(c0 + 4) * 128].rearrange(
                    "p (a b) -> p a b", b=128))
            p.wait("tensor", "Vs")
            for grpi in range(2):
                bk = 6 + grpi
                c0 = grpi * 4
                for j in range(4):
                    h = p.t.transpose(
                        bkbf(bk)[:, j * 128:(j + 1) * 128],
                        scrA[:, 8192 + (c0 + j) * 1024 + i * 128:
                             8192 + (c0 + j) * 1024 + (i + 1) * 128],
                        ident_sb[:])
                h.inc("Ts")
                p.wait("vector", "Ts")
                p.v.tensor_copy(ytk[:, c0 * 128:(c0 + 4) * 128].rearrange(
                    "p (a b) -> p a b", b=128), bkbf(bk)[:, 0:512].rearrange(
                    "p (a b) -> p a b", b=128)).inc("Vs")
            for ii in range(4):
                sl = acc[:, ii * 1024:(ii + 1) * 1024]
                p.v.tensor_scalar(sl, xt_tok[:, 0:1024],
                                  lg[:, i * 24 + ii * 4:i * 24 + ii * 4 + 1],
                                  None, ALU.mult)
                for j in range(1, 4):
                    p.v.scalar_tensor_tensor(
                        sl, xt_tok[:, j * 1024:(j + 1) * 1024],
                        lg[:, i * 24 + ii * 4 + j:i * 24 + ii * 4 + j + 1],
                        sl, ALU.mult, ALU.add)
                p.v.scalar_tensor_tensor(sl, ytk,
                                         lg[:, i * 24 + 20 + ii:i * 24 + 21 + ii],
                                         sl, ALU.mult, ALU.add)
            p.wait("vector", "Dg")
            p.v.tensor_tensor(xbuf[:, i * D:(i + 1) * D], acc,
                              xbuf[:, i * D:(i + 1) * D], ALU.add).inc("Vs")
        if xf_mode == "out":
            p.wait("sync", "Vs")
            dma("sync", xf_io, xbuf[:], "Dy")

    if "p3a" in ph:
        if xn_mode == "in" and "p2" not in ph:
            import_fm(xn_io, "Dy")
        for h in range(NH):
            r = h % 2
            wq_sb = scrA[:, r * 4096:(r + 1) * 4096]
            for q4 in range(4):
                src = wqh_d[h].rearrange("(c p) d -> p c d", p=128)[
                    :, q4 * 8:(q4 + 1) * 8, :]
                wtok = dma("gpsimd", wq_sb.rearrange("p (c d) -> p c d", d=128)[
                    :, q4 * 8:(q4 + 1) * 8, :], src, "Dg")
            p.wait("tensor", "Dg", wtok)
            for c in range(NKC):
                for qt in range(2):
                    hh = p.t.matmul(BK[2 * r + qt][:],
                                    wq_sb[:, c * 128:(c + 1) * 128],
                                    nb(c, qt * 512, 512),
                                    start=(c == 0), stop=(c == NKC - 1),
                                    skip_group_check=True)
            q_sb = scrB[:, r * 1024:(r + 1) * 1024]
            for qt in range(2):
                p.v.tensor_copy(q_sb[:, qt * 512:(qt + 1) * 512], BK[2 * r + qt][:])
                dma("sync", q_io[h * 128:(h + 1) * 128, qt * 512:(qt + 1) * 512],
                    q_sb[:, qt * 512:(qt + 1) * 512], "Dy2")

    if "p3b" in ph:
        if xf_mode == "in":
            dma("sync", xbuf[:], xf_io, "Dy")
        dma("sync", misc[0:1, 40:42], padsc_d, "Dy")
        masks = nbuf[:, 16384:16384 + 4 * QT]
        iot = nbuf[:, 20480:21504].bitcast(I32)
        bexp_sb = fv(nbuf, 10752, 128, NSLOT)
        dma("sync", bexp_sb, bexp_d, "Dy")
        if FUSED_KV:
            p.wait("gpsimd", "Dy")
            p.g.collective_compute(
                "AllGather", ALU.bypass,
                replica_groups=[list(range(NCORE))],
                ins=[k_own_io.opt()], outs=[k_ag_d.opt()]).inc("Dcc", 1)
            p.g.collective_compute(
                "AllGather", ALU.bypass,
                replica_groups=[list(range(NCORE))],
                ins=[v_own_io.opt()], outs=[v_ag_d.opt()]).inc("Dcc", 1)
            p.wait("sync", "Dcc")
            for q in range(16):
                r = q if q < 8 else 15 - q
                off = 0 if q < 8 else 512
                ksrc = k_ag_d[r * 128:(r + 1) * 128, off:off + 512]
                vsrc = v_ag_d[r * 128:(r + 1) * 128, off:off + 512]
                if q < 8:
                    dma("sync", scrA[:, q * 512:(q + 1) * 512], ksrc, "Dy")
                    dma("sync", scrB[:, q * 512:(q + 1) * 512], vsrc, "Dy")
                dma("sync", scrA[:, (LA + 4 * q) * 128:(LA + 4 * q + 4) * 128],
                    ksrc, "Dy")
                dma("sync", scrB[:, (LA + 4 * q) * 128:(LA + 4 * q + 4) * 128],
                    vsrc, "Dy")
            for dd in range(2):
                o = (LA + LB + 4 * dd) * 128
                dma("sync", scrA[:, o:o + 512],
                    k_own_io[:, dd * 512:(dd + 1) * 512], "Dy")
                dma("sync", scrB[:, o:o + 512],
                    v_own_io[:, dd * 512:(dd + 1) * 512], "Dy")
        else:
            for q4 in range(4):
                n4 = NSLOT * 128 // 4
                dma("sync", scrA[:, q4 * n4:(q4 + 1) * n4],
                    k_all_d[:, q4 * n4:(q4 + 1) * n4], "Dy")
                dma("sync", scrB[:, q4 * n4:(q4 + 1) * n4],
                    v_all_d[:, q4 * n4:(q4 + 1) * n4], "Dy")
        kvtok = p.snap("Dy")
        for dd in range(4):
            p.wait("gpsimd", "Vs")
            p.g.iota(iot, [[1, QT]], base=-dd * 128, channel_multiplier=-1).inc("Gs")
            p.wait("vector", "Gs")
            p.v.tensor_scalar(masks[:, dd * QT:(dd + 1) * QT], iot, 0, None,
                              ALU.is_ge).inc("Vs")
        attnout = nbuf[:, 0:16384]
        qrot = [scrA[:, 13312 + r * 512:13312 + (r + 1) * 512] for r in range(2)]
        SCB = [0, 1, 5, 6]
        prot = [sbuf(f"protb{r}", [128, 512], BF16)[:] for r in range(4)]
        reprot = [fv(scrB, 7168 + r * 512, 128, 512) for r in range(2)]
        tmpb = scrA[:, 15360:16384]
        p.wait("tensor", "Dy", kvtok)
        p.wait("vector", "Dy", kvtok)
        for qt in range(2):
            L = LA if qt == 0 else LB
            regoff = 0 if qt == 0 else LA
            diagoff = LA + LB + 4 * qt
            for h in range(NH):
                r = h % 2
                p.wait("sync", "Ts")
                qtok = dma("sync", qrot[r],
                           q_io[h * 128:(h + 1) * 128, qt * 512:(qt + 1) * 512], "Dq")
                p.wait("tensor", "Dq", qtok)
                def slot_of(u):
                    return (regoff + u) if u < L else (diagoff + (u - L))
                for uu in range(L + 6):
                    if uu < L + 4:
                        u = uu
                        slot = slot_of(u)
                        sb = BK[SCB[u % 4]]
                        pb = prot[u % 4]
                        p.t.matmul(sb[:], scrA[:, slot * 128:(slot + 1) * 128],
                                   qrot[r], start=True, stop=True,
                                   skip_group_check=True)
                        p.s.activation(pb, sb[:], AF.Exp, scale=SCALE_ATT,
                                       bias=bexp_sb[:, slot:slot + 1])
                        if u >= L:
                            p.v.tensor_tensor(pb, pb,
                                              masks[:, (u - L) * QT:(u - L + 1) * QT],
                                              ALU.mult)
                    if uu >= 2:
                        u = uu - 2
                        slot = slot_of(u)
                        pb = prot[u % 4]
                        p.t.matmul(BK[2][:], scrB[:, slot * 128:(slot + 1) * 128], pb,
                                   start=(u == 0), stop=(u == L + 3),
                                   skip_group_check=True)
                        hh = p.t.matmul(BK[3][0:1, :], ones_colb[:], pb,
                                        start=(u == 0), stop=(u == L + 3),
                                        skip_group_check=True)
                hh.inc("Ts")
                p.wait("vector", "Ts")
                p.v.tensor_copy(rowf[0:1, 0:512], BK[3][0:1, :])
                if DEBUG_P3B and qt == 0 and h == 0:
                    p.v.tensor_copy(fv(nbuf, 8192, 1, 512), rowf[0:1, 0:512]).inc("Vs")
                p.v.tensor_scalar(rowf[0:1, 0:512], rowf[0:1, 0:512],
                                  misc[0:1, 40 + qt:41 + qt], None, ALU.subtract)
                p.v.reciprocal(rowf[0:1, 0:512], rowf[0:1, 0:512]).inc("Vs")
                if DEBUG_P3B and qt == 0 and h == 0:
                    p.v.tensor_copy(fv(nbuf, 8704, 1, 512), rowf[0:1, 0:512])
                    p.v.tensor_copy(fv(nbuf, 9216, 128, 512), BK[2][:]).inc("Vs")
                p.wait("tensor", "Vs")
                p.t.matmul(BK[4][:], ones_rowf[0:1, 0:128], rowf[0:1, 0:512],
                           start=True, stop=True, skip_group_check=True).inc("Ts")
                p.wait("vector", "Ts")
                p.v.tensor_copy(reprot[r], BK[4][:])
                p.v.tensor_tensor(attnout[:, h * 512:(h + 1) * 512], BK[2][:],
                                  reprot[r], ALU.mult)
                p.lastinc("vector", "Vs")
            worot = [scrA[:, 13312 + r2 * 1024:13312 + (r2 + 1) * 1024]
                     for r2 in range(2)]
            for fg in range(4):
                for h in range(NH):
                    r2 = h % 2
                    p.wait("gpsimd", "Ts")
                    wtok = dma("gpsimd", worot[r2],
                               wo_d[h * 128:(h + 1) * 128, fg * 1024:(fg + 1) * 1024],
                               "Dg")
                    p.wait("tensor", "Dg", wtok)
                    if h < 2:
                        p.wait("tensor", "Vs")
                    for fs in range(8):
                        hh = p.t.matmul(BK[fs][:], worot[r2][:, fs * 128:(fs + 1) * 128],
                                        attnout[:, h * 512:(h + 1) * 512],
                                        start=(h == 0), stop=(h == NH - 1),
                                        skip_group_check=True)
                        if fs == 7:
                            hh.inc("Ts")
                p.wait("vector", "Ts")
                for fs in range(8):
                    p.v.tensor_copy(tmpb[:, 0:512], BK[fs][:]).inc("Vs")
                    p.wait("tensor", "Vs")
                    for qs in range(4):
                        h2 = p.t.transpose(bkbf(fs)[:, qs * 128:(qs + 1) * 128],
                                           tmpb[:, qs * 128:(qs + 1) * 128], ident_sb[:])
                    h2.inc("Ts")
                    p.wait("vector", "Ts")
                    for qs in range(4):
                        sl = xbt(qt * 4 + qs, fg * 1024 + fs * 128, 128)
                        p.v.scalar_tensor_tensor(sl, bkbf(fs)[:, qs * 128:(qs + 1) * 128],
                                                 1.0, sl, ALU.mult, ALU.add)
                    p.lastinc("vector", "Vs")
        if DEBUG_P3B:
            p.wait("sync", "Vs")
            dma("sync", dbg_d[:, 0:2048], fv(nbuf, 8192, 128, 2048), "Dy")
        if x1_mode == "out":
            p.wait("sync", "Vs")
            dma("sync", x1_io, xbuf[:], "Dy")

    if "p4" in ph:
        if x1_mode == "in":
            dma("sync", xbuf[:], x1_io, "Dy")
            p.wait("vector", "Dy")
            p.wait("scalar", "Dy")
            p.wait("tensor", "Dy")
        sqs = scrB[:, 0:4096]
        xn1t = scrB[:, 4096:8192]
        x1st = scrB[:, 8192:8704]
        for i in range(NT):
            mu, rstd = stats_ln(xbuf[:, i * D:(i + 1) * D], sqs, vsave[:, i:i + 1])
            p.wait("vector", "Ts")
            p.v.tensor_scalar(xn1t, xbuf[:, i * D:(i + 1) * D], mu, rstd,
                              ALU.subtract, ALU.mult).inc("Vs")
            p.wait("tensor", "Vs")
            transpose_32(
                lambda c: xn1t[:, c * 128:(c + 1) * 128],
                lambda c0: nbuf[:].rearrange("p (c t) -> p c t", t=TPC)[
                    :, c0:c0 + 4, i * 128:(i + 1) * 128])
            for grpi in range(8):
                bk = 6 + (grpi % 2)
                c0 = grpi * 4
                p.wait("tensor", "Vs")
                for j in range(4):
                    h = p.t.transpose(
                        bkbf(bk)[:, j * 128:(j + 1) * 128],
                        xbuf[:, i * D + (c0 + j) * 128:i * D + (c0 + j + 1) * 128],
                        ident_sb[:])
                h.inc("Ts")
                p.wait("vector", "Ts")
                p.wait("vector", "Dy2")
                p.v.tensor_copy(x1st[:], bkbf(bk)[:, 0:512]).inc("Vs")
                p.wait("sync", "Vs")
                dst = x1f_io.rearrange("(c p) t -> p c t", p=128)[
                    :, c0:c0 + 4, i * 128:(i + 1) * 128]
                dma("sync", dst, x1st[:].rearrange("p (a b) -> p a b", b=128), "Dy2")
        if xn1_mode == "out":
            export_fm(xn1_io, "Dy")

    if "p5" in ph:
        if xn1_mode == "in":
            import_fm(xn1_io, "Dy")
            p.wait("tensor", "Dy")
            p.wait("vector", "Dy")
            p.wait("scalar", "Dy")
            p.wait("gpsimd", "Dy")
        sT = fv(xbuf, 0, 64, 4096)
        zT = fv(xbuf, 4096, 64, 4096)
        mT_b = xbuf[:, 16384:20480]
        dma("sync", sT, scaleT_d, "Dy")
        dma("sync", zT, zpT_d, "Dy")
        p.wait("vector", "Dy")
        p.v.tensor_tensor(sT, sT, zT, ALU.mult)
        p.v.tensor_copy(mT_b[0:64, :], sT).inc("Vs")
        p.v.memset(ew[:], 0.0)
        p.v.memset(ew[0:64, 64:65], 1.0)
        p.v.memset(ew[64:128, 65:66], 1.0).inc("Vs")
        dma("gpsimd", g2[:], g2_d, "Dg")
        for c in range(NKC):
            dma("sync", bcol[:, c:c + 1], bq1_d[c * 128:(c + 1) * 128, :], "Dy")
            dma("sync", bcol[:, 32 + c:33 + c], bk1_d[c * 128:(c + 1) * 128, :], "Dy")
            dma("sync", bcol[:, 64 + c:65 + c], bias1_d[c * 128:(c + 1) * 128, :], "Dy")
        btok = p.snap("Dy")
        DO_P = "p" in P5_PARTS
        DO_F = "f" in P5_PARTS
        DO_A = "a" in P5_PARTS
        wbig = [scrA[:, 0:12288], xbuf[:, 0:12288]]
        qst = scrB[:, 0:1024]
        kst = scrB[:, 1024:2048]
        vst = scrB[:, 2048:3072]
        vtr = scrB[:, 3072:3584]
        p.wait("vector", "Dy", btok)
        for fc in range(NKC if DO_P else 0):
            r = fc % 2
            for q4 in range(4):
                sl = slice(q4 * 8, (q4 + 1) * 8)
                src = wq1h_d[fc].rearrange("(c p) d -> p c d", p=128)[:, sl, :]
                dma("gpsimd", wbig[r][:, 0:4096].rearrange(
                    "p (c d) -> p c d", d=128)[:, sl, :], src, "Dg4")
                src = wk1h_d[fc].rearrange("(c p) d -> p c d", p=128)[:, sl, :]
                dma("gpsimd", wbig[r][:, 4096:8192].rearrange(
                    "p (c d) -> p c d", d=128)[:, sl, :], src, "Dg4")
                src = wv1h_d[fc].rearrange("(c p) d -> p c d", p=128)[:, sl, :]
                wtok = dma("gpsimd", wbig[r][:, 8192:12288].rearrange(
                    "p (c d) -> p c d", d=128)[:, sl, :], src, "Dg4")
            p.wait("tensor", "Dg4", wtok)
            for c in range(NKC):
                for th in range(2):
                    rhs = nb(c, th * 512, 512)
                    p.t.matmul(BK[0 + th][:], wbig[r][:, c * 128:(c + 1) * 128], rhs,
                               start=(c == 0), stop=(c == NKC - 1), skip_group_check=True)
                    p.t.matmul(BK[2 + th][:], wbig[r][:, 4096 + c * 128:4096 + (c + 1) * 128],
                               rhs, start=(c == 0), stop=(c == NKC - 1),
                               skip_group_check=True)
                    hh = p.t.matmul(BK[4 + th][:], wbig[r][:, 8192 + c * 128:8192 + (c + 1) * 128],
                                    rhs, start=(c == 0), stop=False, skip_group_check=True)
                hh.inc("Ts")
            p.wait("gpsimd", "Ts")
            brtok = dma("gpsimd", brow[0:1, 0:128],
                        bv1_d[0:1, fc * 128:(fc + 1) * 128], "Dg4")
            p.wait("tensor", "Dg4", brtok)
            for th in range(2):
                hh = p.t.matmul(BK[4 + th][:], brow[0:1, 0:128], ones_rowb[0:1, 0:512],
                                start=False, stop=True, skip_group_check=True)
            hh.inc("Ts")
            p.wait("vector", "Ts")
            p.wait("vector", "Dq")
            for th in range(2):
                p.v.tensor_scalar(qst[:, th * 512:(th + 1) * 512], BK[0 + th][:],
                                  bcol[:, fc:fc + 1], None, ALU.add)
                p.v.tensor_scalar(kst[:, th * 512:(th + 1) * 512], BK[2 + th][:],
                                  bcol[:, 32 + fc:33 + fc], None, ALU.add)
                p.v.tensor_copy(vst[:, th * 512:(th + 1) * 512], BK[4 + th][:])
            p.lastinc("vector", "Vs")
            p.wait("sync", "Vs")
            dma("sync", q1_d[fc * 128:(fc + 1) * 128, :], qst, "Dq")
            dma("sync", k1_d[fc * 128:(fc + 1) * 128, :], kst, "Dq")
            for grpi in range(2):
                bk = 6 + grpi
                p.wait("tensor", "Vs")
                for j in range(4):
                    i4 = grpi * 4 + j
                    h = p.t.transpose(bkbf(bk)[:, j * 128:(j + 1) * 128],
                                      vst[:, i4 * 128:(i4 + 1) * 128], ident_sb[:])
                h.inc("Ts")
                p.wait("vector", "Ts")
                p.wait("vector", "Dq")
                p.v.tensor_copy(vtr[:], bkbf(bk)[:, 0:512]).inc("Vs")
                p.wait("sync", "Vs")
                dst = v1t_d.rearrange("p (i f) -> p i f", f=D)[
                    :, grpi * 4:(grpi + 1) * 4, fc * 128:(fc + 1) * 128]
                dma("sync", dst, vtr[:].rearrange("p (a b) -> p a b", b=128), "Dq")
        negs = xbuf[0:64, 20480:22528]
        sneg = scrB[:, 0:64]
        for ts_ in range(NT if DO_F else 0):
            p.wait("tensor", "Vs")
            for c in range(NKC):
                hh = p.t.matmul(BK[6][:, 0:64], nb(c, ts_ * 128, 128),
                                ew[:, 64 - 2 * c:128 - 2 * c],
                                start=(c == 0), stop=(c == NKC - 1),
                                skip_group_check=True)
            hh.inc("Ts")
            p.wait("vector", "Ts")
            p.v.tensor_scalar(sneg, BK[6][:, 0:64], -1.0, None, ALU.mult).inc("Vs")
            p.wait("tensor", "Vs")
            p.t.transpose(bkbf(7)[0:64, 0:128], sneg, ident_sb[:]).inc("Ts")
            p.wait("vector", "Ts")
            p.v.tensor_copy(negs[0:64, ts_ * 128:(ts_ + 1) * 128],
                            bkbf(7)[0:64, 0:128]).inc("Vs")
    def iv(tensor, off_i32, pdim, n):
        return tensor[0:pdim, 2 * off_i32:2 * (off_i32 + n)].bitcast(I32)
    if "p5" in ph:
        qint = [iv(xbuf, r * 2048, 128, 2048) for r in range(2)]
        srep8 = [fv(xbuf, 4096 + r * 2048, 128, 2048) for r in range(2)]
        deq8 = [scrA[:, 5632 + r * 2048:5632 + (r + 1) * 2048] for r in range(2)]
        x1rot = [scrA[:, 13824 + r * 512:13824 + (r + 1) * 512] for r in range(2)]
        ob32 = [fv(scrB, 0, 128, 512), fv(scrB, 512, 128, 512)]
        facc = [fv(scrB, 5120 + i * 512, 128, 512) for i in range(4)]
        for ng in range(16):
            n0 = ng * 256
            for sub in range(4):
                r = sub % 2
                c0 = sub * 8
                src = wTq1_d.rearrange("(c p) n -> p c n", p=128)[
                    :, c0:c0 + 8, n0:n0 + 256]
                dma("sync", qint[r].rearrange("p (c n) -> p c n", n=256),
                    src, "Dg5")
                src = sexp_d.rearrange("(c p) n -> p c n", p=128)[
                    :, c0:c0 + 8, n0:n0 + 256]
                dma("sync", srep8[r].rearrange("p (c n) -> p c n", n=256),
                    src, "Dg5")
                sptok = p.snap("Dg5")
                p.wait("vector", "Dg5", sptok)
                p.wait("vector", "Ts")
                p.v.tensor_tensor(deq8[r], qint[r], srep8[r], ALU.mult).inc("Vs")
                if DEBUG_P5 and ng == 15 and sub == 3:
                    p.v.tensor_copy(scrB[:, 2048:4096], qint[r])
                    p.v.tensor_copy(scrB[:, 4096:6144], srep8[r])
                    p.v.tensor_copy(scrB[:, 6144:8192], deq8[r]).inc("Vs")
                p.wait("tensor", "Vs")
                for c8 in range(8):
                    for ns in range(2):
                        for th in range(2):
                            hh = p.t.matmul(
                                BK[ns * 2 + th][:],
                                deq8[r][:, c8 * 256 + ns * 128:c8 * 256 + (ns + 1) * 128],
                                nb(sub * 8 + c8, th * 512, 512),
                                start=(c8 == 0), stop=(c8 == 7),
                                skip_group_check=True)
                hh.inc("Ts")
                p.wait("vector", "Ts")
                for ns in range(2):
                    for th in range(2):
                        i4 = ns * 2 + th
                        if sub == 0:
                            p.v.tensor_copy(facc[i4], BK[i4][:])
                        else:
                            p.v.scalar_tensor_tensor(facc[i4], BK[i4][:], 1.0,
                                                     facc[i4], ALU.mult, ALU.add)
                p.lastinc("vector", "Vs")
                p.wait("tensor", "Vs")
            for ns in range(2):
                nchunk = ng * 2 + ns
                for th in range(2):
                    p.wait("tensor", "Vs")
                    hh = p.t.matmul(BK[4 + th][:],
                                    mT_b[0:64, nchunk * 128:(nchunk + 1) * 128],
                                    negs[0:64, th * 512:(th + 1) * 512],
                                    start=True, stop=True, skip_group_check=True)
                    hh.inc("Ts")
                    r = th
                    p.wait("sync", "Vs")
                    xtok = dma("sync", x1rot[r],
                               x1f_io.rearrange("(c p) t -> p c t", p=128)[
                                   :, nchunk:nchunk + 1, th * 512:(th + 1) * 512
                               ].squeeze(1), "Dy2")
                    p.wait("vector", "Ts")
                    p.wait("vector", "Dy2", xtok)
                    p.wait("vector", "Do")
                    i4 = ns * 2 + th
                    p.v.scalar_tensor_tensor(ob32[r], BK[4 + th][:], 1.0,
                                             facc[i4], ALU.mult, ALU.add)
                    p.v.tensor_scalar(ob32[r], ob32[r],
                                      bcol[:, 64 + nchunk:65 + nchunk], None, ALU.add)
                    p.v.scalar_tensor_tensor(ob32[r], x1rot[r], 1.0, ob32[r],
                                             ALU.mult, ALU.add).inc("Vs")
                    p.wait("sync", "Vs")
                    dma("sync", out_d[nchunk * 128:(nchunk + 1) * 128,
                                      th * 512:(th + 1) * 512], ob32[r], "Do")
        ctx = fv(xbuf, 0, 128, 8192)
        vhm = xbuf[0:32, 22528:30720]
        prob_f = fv(scrB, 4096, 32, 512)
        ssum = fv(scrB, 4608, 32, 16)
        prob_b = scrB[0:32, 9248:9760]
        probT = scrB[0:32, 9760:10272]
        for gr in range(4 if DO_A else 0):
            t0 = gr * 256
            p.wait("sync", "Ts")
            for q4 in range(4):
                src = q1_d.rearrange("(c p) t -> p c t", p=128)[
                    :, q4 * 8:(q4 + 1) * 8, t0:t0 + 256]
                dma("sync", scrA[:, 0:8192].rearrange("p (c t) -> p c t", t=256)[
                    :, q4 * 8:(q4 + 1) * 8, :], src, "Dq")
                src = k1_d.rearrange("(c p) t -> p c t", p=128)[
                    :, q4 * 8:(q4 + 1) * 8, t0:t0 + 256]
                dma("sync", scrA[:, 8192:16384].rearrange("p (c t) -> p c t", t=256)[
                    :, q4 * 8:(q4 + 1) * 8, :], src, "Dq")
            p.wait("sync", "Dg2")
            dma("sync", scrB[:, 0:8192], v1t_d[:, (gr * 2) * D:(gr * 2 + 2) * D], "Dq")
            gtok = p.snap("Dq")
            p.wait("tensor", "Dq", gtok)
            p.wait("gpsimd", "Dq", gtok)
            p.wait("vector", "Da")
            for mc in range(4):
                p.wait("gpsimd", "Ts")
                for h2 in range(NH):
                    src = scrB[(mc % 2) * 64:(mc % 2) * 64 + 64,
                               (mc // 2) * D + h2 * 128:(mc // 2) * D + (h2 + 1) * 128]
                    dma("gpsimd", vhm[h2:h2 + 1, :], src, "Dg2")
                vtok_ = p.snap("Dg2")
                p.wait("tensor", "Dg2", vtok_)
                for b16 in range(4):
                    sb = BK[b16 % 2]
                    cb = BK[2 + (b16 % 2)]
                    p.wait("tensor", "Ss")
                    p.wait("tensor", "Vs")
                    for tt in range(16):
                        tl = mc * 64 + b16 * 16 + tt
                        qap = scrA[:, 0:8192].rearrange("p (c t) -> p c t", t=256)[
                            :, :, tl:tl + 1].squeeze(2)
                        kap = scrA[:, 8192:16384].rearrange("p (c t) -> p c t", t=256)[
                            :, :, tl:tl + 1].squeeze(2)
                        hh = p.t.matmul(sb[0:32, tt * 32:(tt + 1) * 32], qap, kap,
                                        start=True, stop=True, skip_group_check=True)
                    hh.inc("Ts")
                    p.wait("scalar", "Ts")
                    p.wait("scalar", "Vs")
                    p.s.activation(prob_f, sb[0:32, :], AF.Exp, scale=SCALE_ATT,
                                   bias=cexp[0:32, 1:2]).inc("Ss")
                    p.wait("vector", "Ss")
                    p.v.reduce_sum(ssum.rearrange("p (t o) -> p t o", o=1),
                                   prob_f.rearrange("p (t h) -> p t h", h=32),
                                   axis=AX.X)
                    p.v.reciprocal(ssum, ssum)
                    p.v.tensor_tensor(prob_b.rearrange("p (t h) -> p t h", h=32),
                                      prob_f.rearrange("p (t h) -> p t h", h=32),
                                      ssum.rearrange("p (t o) -> p t o",
                                                     o=1).broadcast_to([32, 16, 32]),
                                      ALU.mult)
                    p.v.transpose(probT, prob_b).inc("Vs")
                    p.wait("tensor", "Vs")
                    for tt in range(16):
                        tl = mc * 64 + b16 * 16 + tt
                        hh = p.t.matmul(cb[:, tt * 32:(tt + 1) * 32],
                                        vhm[:, (tl % 64) * 128:((tl % 64) + 1) * 128],
                                        probT[:, tt * 32:(tt + 1) * 32],
                                        start=True, stop=True, skip_group_check=True)
                    hh.inc("Ts")
                    p.wait("vector", "Ts")
                    dst = ctx.rearrange("p (h t) -> p h t", t=256)[
                        :, :, mc * 64 + b16 * 16:mc * 64 + b16 * 16 + 16]
                    p.v.tensor_copy(dst, cb[:].rearrange("p (t h) -> p h t",
                                                         h=32)).inc("Vs")
            p.wait("gpsimd", "Vs")
            p.wait("gpsimd", "Do")
            if DEBUG_P5:
                dma("gpsimd", dbg5_d[:, gr * 8192:(gr + 1) * 8192].rearrange(
                    "p n -> p n"), fv(xbuf, 0, 128, 8192), "Da")
            for h in range(NH):
                src = ctx[:, h * 256:(h + 1) * 256]
                dma("gpsimd", out_d[h * 128:(h + 1) * 128, t0:t0 + 256], src,
                    "Da", accum_op=ALU.add)

    if "p5" in ph and DEBUG_P5:
        p.wait("sync", "Vs")
        dma("sync", dbg6_d[:, 0:6144], scrB[:, 2048:8192], "Dy")
    for sem in list(p.count.keys()):
        if sem.startswith("D"):
            p.wait("sync", sem)

    p.build(nc, es)
    return nc, es, ins, outs


def kernel(**inputs):
    pre = prep_host(inputs)
    shared = {k: pre[k] for k in
              ("wk", "wv", "w_qh", "wo", "wcat", "basecat", "gate", "wbr",
               "wq1h", "wk1h", "wv1h", "bq1", "bk1", "bv1", "wTq1", "sexp", "g2in",
               "scaleT", "zpT", "bias1", "ident")}
    percore = {"x_own": pre["x_own"], "padsc": pre["padsc"], "bexp": pre["bexp"]}

    for group in PLAN:
        nc, es, ins, outs = emit_program(group)
        in_maps = []
        for c in range(NCORE):
            m = {}
            for name in ins:
                if name in shared:
                    m[name] = shared[name]
                elif name in percore:
                    m[name] = percore[name][c]
                else:
                    raise KeyError(f"missing input {name}")
            in_maps.append(m)
        res = run_bass_kernel_spmd(nc, in_maps, core_ids=list(range(NCORE)))
        es.__exit__(None, None, None)
        for name in outs:
            percore[name] = [res.results[c][name] for c in range(NCORE)]
        if "k_own" in outs:
            k_all, v_all = assemble_kv(percore["k_own"], percore["v_own"])
            percore["k_all"] = k_all
            percore["v_all"] = v_all

    out_T = np.zeros((D, T), dtype=np.float32)
    for qt in range(16):
        c = qt if qt < 8 else 15 - qt
        lo = 0 if qt < 8 else 512
        out_T[:, qt * 512:(qt + 1) * 512] = \
            np.asarray(percore["out"][c], np.float32)[:, lo:lo + 512]
    return np.ascontiguousarray(out_T.T).reshape(1, T, D)
